# version 31
# baseline (speedup 1.0000x reference)
import contextlib

import numpy as np

import concourse.bass as bass
import concourse.mybir as mybir
from concourse.bass_utils import run_bass_kernel_spmd

F32 = mybir.dt.float32
BF16 = mybir.dt.bfloat16
I32 = mybir.dt.int32
Act = mybir.ActivationFunctionType
Alu = mybir.AluOpType
AX = mybir.AxisListType.X

T = 4096
D = 1024
DI = 3072
DC = D // 128
ICN = DI // 128
QB = 512
NQ = T // QB
NT128 = T // 128
NT512 = T // 512

DEBUG_TAPS = False
DEBUG_SERIAL = False


class Sched:

    def __init__(self):
        self.items = []
        self.cnt = {}

    def reg(self, name):
        self.cnt[name] = 0

    def do(self, eng, fn, incs=()):
        for s, n in incs:
            self.cnt[s] += n
        self.items.append((eng, "op", (fn, tuple(incs))))

    def wait(self, eng, sem, val=None):
        if val is None:
            val = self.cnt[sem]
        assert val <= self.cnt[sem], (sem, val, self.cnt[sem])
        if val > 0:
            self.items.append((eng, "wait", (sem, val)))

    def now(self, sem):
        return self.cnt[sem]


def rep_ap(col_ap, n_outer, n_rep):
    return bass.AP(col_ap.tensor, col_ap.offset,
                   [col_ap.ap[0], [1, n_outer], [0, n_rep]])


def build():
    nc = bass.Bass("TRN2", target_bir_lowering=False,
                   detect_race_conditions=False)
    S = Sched()
    for nm in ("LD", "S2", "GD", "CC", "TE", "VE", "SE", "GE"):
        S.reg(nm)

    def dram(name, shape, dtype):
        kind = "ExternalOutput" if DEBUG_TAPS else "Internal"
        return nc.dram_tensor(name, shape, dtype, kind=kind)

    x_h = nc.dram_tensor("x", [T, D], F32, kind="ExternalInput").ap()
    w_h = {}
    for nm, shp in (("wq", [D, D]), ("wk", [D, D]), ("wv", [D, D]),
                    ("wo", [D, D]), ("wgate", [DI, D]), ("wup", [DI, D]),
                    ("wdown", [D, DI])):
        w_h[nm] = nc.dram_tensor(nm, shp, F32, kind="ExternalInput").ap()
    b_h = {nm: nc.dram_tensor(nm, [1, D], F32, kind="ExternalInput").ap()
           for nm in ("bq", "bk", "bv", "bo")}
    out_h = nc.dram_tensor("out", [T, D], F32, kind="ExternalOutput").ap()

    sLNX = dram("sLNX", [T, D], BF16).ap()
    sQ = dram("sQ", [D, T], BF16).ap()
    sK = dram("sK", [D, T], BF16).ap()
    sV = dram("sV", [T, D], BF16).ap()
    sX1 = dram("sX1", [T, D], F32).ap()
    sXQ = dram("sXQ", [T, D], BF16).ap()
    sATO = dram("sATO", [D, T], BF16).ap()
    sUP = dram("sUP", [DI, T], F32).ap()
    sGA = dram("sGA", [DI, T], F32).ap()
    sOT = dram("sOT", [DI, T], F32).ap()
    sWB = {nm: dram("sWB" + nm, list(w_h[nm].shape), BF16).ap()
           for nm in w_h}
    p2f = nc.dram_tensor("p2f", [1, 128], F32).ap()
    ccI = {k: nc.dram_tensor(f"cc{k}i", [128, c], F32)
           for k, c in ((1, 1), (2, 2), (3, 1), (4, 1))}
    ccO = {k: nc.dram_tensor(f"cc{k}o", [128, c], F32)
           for k, c in ((1, 1), (2, 2), (3, 1), (4, 1))}

    stack = contextlib.ExitStack()
    nc._keepalive_stack = stack

    def sb(name, shape, dtype):
        return stack.enter_context(nc.sbuf_tensor(name, shape, dtype))

    def psum(name):
        return stack.enter_context(nc.psum_tensor(name, [128, 512], F32))

    cst = sb("cst", [128, 4], F32)
    ones1 = sb("ones1", [1, 128], F32)
    id1 = sb("id1", [1, 1], F32)
    onecol = sb("onecol", [128, 1], BF16)
    bqF = sb("bqF", [128, DC], F32)
    bkF = sb("bkF", [128, DC], F32)
    mcol = sb("mcol", [128, NT128], F32)
    statMV4 = sb("statMV4", [128, 4 * NT128], F32)
    maxC = sb("maxC", [128, NT128], F32)
    minC = sb("minC", [128, NT128], F32)
    statM2 = sb("statM2", [128, NT128], F32)
    statV2 = sb("statV2", [128, NT128], F32)
    bscr = sb("bscr", [128, 3 * NT128], F32)
    statR = sb("statR", [128, NT128], F32)
    aB = sb("aB", [128, NT128], F32)
    bBv = sb("bBv", [128, NT128], F32)
    rmaxC = sb("rmaxC", [128, NT128], F32)
    rupC = sb("rupC", [128, 2 * ICN], F32)
    rgaC = sb("rgaC", [128, 2 * ICN], F32)
    rthC = sb("rthC", [128, 2 * ICN], F32)
    rotC = sb("rotC", [128, 2 * ICN], F32)
    smax = sb("smax", [128, 2], F32)
    armaxs = sb("armaxs", [128, 2], F32)
    row1 = sb("row1", [1, 128], F32)
    sc1 = sb("sc1", [1, 24], F32)
    scB = sb("scB", [128, 10], F32)
    lnst = sb("lnst", [128, 12], F32)
    lnag = sb("lnag", [128, 4], F32)
    lnsc = sb("lnsc", [128, 8], F32)
    lntmp = sb("lntmp", [128, 2], F32)

    pA = psum("pA")
    pB = psum("pB")
    pD = psum("pD")
    pC = psum("pC")
    pE = psum("pE")
    pF = psum("pF")
    pG = psum("pG")
    pM = psum("pM")

    sems = {nm: stack.enter_context(nc.semaphore(nm))
            for nm in ("LD", "S2", "GD", "CC", "TE", "VE", "SE", "GE")}

    def op(eng, fn, incs=()):
        S.do(eng, fn, incs)

    def w8(eng, sem, val=None):
        S.wait(eng, sem, val)

    def vhop():
        op("vector", lambda e: e.engine_nop(), [("VE", 1)])
        w8("vector", "VE")

    def dma(q, out, in_, sem, slow=False):
        if slow:
            op(q, lambda e, o=out, i=in_: e.dma_start(
                out=o, in_=i, allow_slow_non_contiguous=True), [(sem, 16)])
        else:
            op(q, lambda e, o=out, i=in_: e.dma_start(out=o, in_=i),
               [(sem, 16)])
        return S.now(sem)

    def dmaT(q, out, in_, sem):
        op(q, lambda e, o=out, i=in_: e.dma_start_transpose(o, i),
           [(sem, 16)])
        return S.now(sem)

    def mm(out, lhsT, rhs, start, stop, inc=False):
        incs = [("TE", 1)] if inc else ()
        op("tensor", lambda e, o=out, l=lhsT, r=rhs, st=start, sp=stop:
           e.matmul(o, l, r, start=st, stop=sp), incs)
        return S.now("TE")

    bankg = {}

    def bank_wait(bank):
        g = bankg.get(bank)
        if g is not None:
            w8("tensor", g[0], g[1])

    def bank_done(bank, sem):
        bankg[bank] = (sem, S.now(sem))

    def bcast(src_cols, dst_tile):
        n = src_cols.shape[-1]
        bank_wait("pM")
        w8("tensor", "VE")
        mm(pM[:, 0:n], ones1[:, :], src_cols, True, True, inc=True)
        w8("vector", "TE")
        op("vector", lambda e, d=dst_tile, n=n: e.tensor_copy(d, pM[:, 0:n]),
           [("VE", 1)])
        bank_done("pM", "VE")

    def cross_part_max(src_col, dst_col):
        dma("gpsimd", p2f.rearrange("a b -> b a"), src_col, "GD")
        w8("gpsimd", "GD")
        dma("gpsimd", row1[:, :], p2f, "GD")
        w8("vector", "GD")
        op("vector", lambda e, d=dst_col: e.tensor_reduce(
            d, row1[:, :], AX, Alu.max), [("VE", 1)])
        vhop()

    def collective(k):
        op("gpsimd", lambda e, k=k: e.collective_compute(
            "AllReduce", Alu.max, replica_groups=[list(range(8))],
            ins=[ccI[k].ap().opt()], outs=[ccO[k].ap().opt()]), [("CC", 1)])
        return S.now("CC")

    def marks():
        return [(s, S.now(s)) for s in ("TE", "VE", "SE", "S2", "GD")]

    def qgate(mk, engines):
        for e in engines:
            for semnm, val in mk:
                w8(e, semnm, val)

    markP = nc.sbuf_base
    wpr = [sb(f"wpr{i}", [128, 1024], F32) for i in range(2)]
    wqi = sb("wqi", [128, 1024], I32)
    wps = sb("wps", [128, 96], F32)
    brow = sb("brow", [1, 512], F32)
    markEA = nc.sbuf_base

    wstore = {}

    def w_prep_chunk(nm, ci, dequant, sec=0):
        buf = (ci + sec) % 2
        key = buf
        if key in wstore:
            w8("sync", "GD", wstore[key])
        src = w_h[nm].rearrange("(c p) i -> p c i", p=128)[
            :, ci:ci + 1, sec * 1024:(sec + 1) * 1024]
        ld = dma("sync", wpr[buf][:, :].rearrange("p (c i) -> p c i", c=1),
                 src, "LD")
        dst = sWB[nm].rearrange("(c p) i -> p c i", p=128)[
            :, ci:ci + 1, sec * 1024:(sec + 1) * 1024]
        if dequant:
            w8("vector", "LD", ld)
            w3 = wpr[buf][:, :].rearrange("p (b e) -> p b e", e=32)
            q3 = wqi[:, :].rearrange("p (b e) -> p b e", e=32)
            op("vector", lambda e, w3=w3: e.tensor_reduce(
                wps[:, 0:32], w3, AX, Alu.max, apply_absolute_value=True))
            vhop()
            op("vector", lambda e: e.tensor_scalar(
                wps[:, 32:64], wps[:, 0:32], 1.0 / 7.0, 1e-12,
                Alu.mult, Alu.max))
            vhop()
            op("vector", lambda e: e.reciprocal(wps[:, 64:96],
                                                wps[:, 32:64]))
            vhop()
            inv_bc = rep_ap(wps[:, 64:65], 32, 32)
            s_bc = rep_ap(wps[:, 32:33], 32, 32)
            op("vector", lambda e, w3=w3, q3=q3, bc=inv_bc: e.tensor_tensor(
                q3, w3, bc, Alu.mult))
            op("vector", lambda e, w3=w3, q3=q3, bc=s_bc: e.tensor_tensor(
                w3, q3, bc, Alu.mult), [("VE", 1)])
            w8("gpsimd", "VE")
        else:
            w8("gpsimd", "LD", ld)
        wstore[key] = dma(
            "gpsimd", dst,
            wpr[buf][:, :].rearrange("p (c i) -> p c i", c=1), "GD")

    def wT_load(nm, dstT, gd_ready):
        rows, cols = sWB[nm].shape
        w8("scalar", "GD", gd_ready)
        for c in range(cols // 128):
            dmaT("scalar", dstT[:, c * rows:(c + 1) * rows],
                 sWB[nm][:, c * 128:(c + 1) * 128], "S2")
        return S.now("S2")

    def mset(e):
        e.memset(cst[:, 0:1], 1.0 / 1024.0)
        e.memset(cst[:, 1:2], -2.0)
        e.memset(cst[:, 2:3], -0.5)
        e.memset(cst[:, 3:4], 1e-5)
        e.memset(ones1[:, :], 1.0)
        e.memset(id1[:, :], 1.0)
        return e.memset(onecol[:, :], 1.0)
    op("vector", mset, [("VE", 1)])

    op("gpsimd", lambda e: e.iota(
        statMV4[:, 0:NT128], [[128, NT128]], channel_multiplier=1,
        allow_small_or_imprecise_dtypes=True), [("GE", 1)])
    w8("scalar", "GE")
    w8("scalar", "VE")
    op("scalar", lambda e: e.activation(statMV4[:, 0:NT128], statMV4[:, 0:NT128], Act.Square,
                                        bias=cst[:, 1:2], scale=cst[:, 0:1]),
       [("SE", 1)])
    w8("vector", "SE")
    op("vector", lambda e: e.tensor_scalar(statMV4[:, 0:NT128], statMV4[:, 0:NT128],
                                           -0.5, None, Alu.mult),
       [("VE", 1)])
    w8("scalar", "VE")
    op("scalar", lambda e: e.activation(mcol[:, :], statMV4[:, 0:NT128], Act.Exp,
                                        bias=0.0, scale=1.0), [("SE", 1)])
    w8("vector", "SE")
    op("vector", lambda e: e.tensor_scalar(mcol[:, :], mcol[:, :],
                                           1.0 / 32.0, None, Alu.mult),
       [("VE", 1)])

    for bh, dst in ((b_h["bq"], bqF), (b_h["bk"], bkF)):
        bias_ld = dma("sync", dst[:, :],
                      bh.rearrange("one (c p) -> p (one c)", p=128),
                      "LD", slow=True)
    w8("scalar", "LD", bias_ld)

    LNxT = sb("LNxT", [128, DC * T], BF16)
    WqT = sb("WqT", [128, DC * D], BF16)
    WkT = sb("WkT", [128, DC * D], BF16)
    WvT = sb("WvT", [128, DC * D], BF16)
    bvB = sb("bvB", [128, D], F32)
    xr = [sb(f"xr{i}", [128, D], F32) for i in range(2)]
    lnstg = [sb(f"lnstg{i}", [128, D], BF16) for i in range(2)]
    qstg = sb("qstg", [128, DC * 512], BF16)
    kstg = sb("kstg", [128, DC * 512], BF16)
    vstg = [sb(f"vstg{i}", [128, D], BF16) for i in range(2)]

    def LNxTv(c):
        return LNxT[:, c * T:(c + 1) * T]

    def WTv(t, c):
        return t[:, c * D:(c + 1) * D]

    w8("tensor", "VE", 1)
    for h in range(2):
        if h:
            w8("sync", "TE")
        ld_bv = dma("sync", brow[:, :], b_h["bv"][:, h * 512:(h + 1) * 512],
                    "LD")
        w8("tensor", "LD", ld_bv)
        bank_wait("pM")
        mm(pM[:, 0:512], ones1[:, :], brow[:, :],
           True, True, inc=True)
        w8("vector", "TE")
        op("vector", lambda e, h=h: e.tensor_copy(
            bvB[:, h * 512:(h + 1) * 512], pM[:, 0:512]), [("VE", 1)])
        bank_done("pM", "VE")

    ln_done = [None] * NT128
    xr_free = [None, None]
    lnxT_ready = [0] * 4
    qk_stg_free = {"q": None, "k": None}
    vstg_free = [None, None]
    ln_sqrt_se = [None]

    def ln1_tile(tt):
        buf = tt % 2
        if xr_free[buf] is not None:
            w8("sync", "VE", xr_free[buf])
        ld = dma("sync", xr[buf][:, :], x_h[tt * 128:(tt + 1) * 128, :],
                 "LD")
        w8("vector", "LD", ld)
        if tt >= 2:
            w8("vector", "S2", ln_done[tt - 2])
        if ln_sqrt_se[0] is not None:
            w8("vector", "SE", ln_sqrt_se[0])
        op("vector", lambda e, b=buf: e.bn_stats(lnst[:, 0:6],
                                                 xr[b][:, 0:512]))
        op("vector", lambda e, b=buf: e.bn_stats(lnst[:, 6:12],
                                                 xr[b][:, 512:1024]))
        op("vector", lambda e: e.bn_aggr(lnag[:, 0:2], lnst[:, 0:6]))
        op("vector", lambda e: e.bn_aggr(lnag[:, 2:4], lnst[:, 6:12]))
        vhop()
        op("vector", lambda e: e.tensor_tensor(lnsc[:, 0:1], lnag[:, 0:1],
                                               lnag[:, 2:3], Alu.add))
        op("vector", lambda e: e.tensor_tensor(lnsc[:, 1:2], lnag[:, 0:1],
                                               lnag[:, 2:3], Alu.subtract))
        op("vector", lambda e: e.tensor_tensor(lnsc[:, 2:3], lnag[:, 1:2],
                                               lnag[:, 3:4], Alu.add))
        vhop()
        op("vector", lambda e: e.tensor_scalar(lnsc[:, 3:4], lnsc[:, 0:1],
                                               0.5, None, Alu.mult))
        op("vector", lambda e: e.tensor_tensor(lnsc[:, 4:5], lnsc[:, 1:2],
                                               lnsc[:, 1:2], Alu.mult))
        op("vector", lambda e: e.tensor_scalar(lnsc[:, 5:6], lnsc[:, 2:3],
                                               0.5, None, Alu.mult))
        vhop()
        op("vector", lambda e: e.scalar_tensor_tensor(
            lnsc[:, 6:7], lnsc[:, 4:5], 0.25, lnsc[:, 5:6],
            Alu.mult, Alu.add), [("VE", 1)])
        w8("scalar", "VE")
        op("scalar", lambda e: e.activation(lntmp[:, 0:1], lnsc[:, 6:7],
                                            Act.Sqrt, bias=cst[:, 3:4]),
           [("SE", 1)])
        ln_sqrt_se[0] = S.now("SE")
        w8("vector", "SE")
        op("vector", lambda e: e.reciprocal(lntmp[:, 1:2], lntmp[:, 0:1]))
        vhop()
        op("vector", lambda e, b=buf: e.tensor_scalar(
            lnstg[b][:, :], xr[b][:, :], lnsc[:, 3:4], lntmp[:, 1:2],
            Alu.subtract, Alu.mult), [("VE", 1)])
        xr_free[buf] = S.now("VE")
        w8("scalar", "VE")
        ln_done[tt] = dma("scalar", sLNX[tt * 128:(tt + 1) * 128, :],
                          lnstg[buf][:, :], "S2")

    def lnxT_quarter(qr):
        w8("scalar", "S2", ln_done[qr * 8 + 7])
        for c in range(DC):
            dmaT("scalar", LNxTv(c)[:, qr * 1024:(qr + 1) * 1024],
                 sLNX[qr * 1024:(qr + 1) * 1024, c * 128:(c + 1) * 128],
                 "S2")
        lnxT_ready[qr] = S.now("S2")

    def qkv_t512(i, wT_gate):
        w8("tensor", "S2", max(lnxT_ready[i // 2], wT_gate))
        for which in ("k", "q"):
            wT = WkT if which == "k" else WqT
            bF = bkF if which == "k" else bqF
            stg = kstg if which == "k" else qstg
            sdst = sK if which == "k" else sQ
            if qk_stg_free[which] is not None:
                w8("scalar", "S2", qk_stg_free[which])
            for oc in range(DC):
                bank, pbank = (("pA", pA) if oc % 2 == 0 else ("pB", pB))
                bank_wait(bank)
                for icc in range(DC):
                    mm(pbank[:, 0:512],
                       WTv(wT, icc)[:, oc * 128:(oc + 1) * 128],
                       LNxTv(icc)[:, i * 512:(i + 1) * 512],
                       icc == 0, icc == DC - 1, inc=(icc == DC - 1))
                w8("scalar", "TE")
                op("scalar", lambda e, oc=oc, pb=pbank, st=stg, bf=bF:
                   e.activation(st[:, oc * 512:(oc + 1) * 512], pb[:, 0:512],
                                Act.Identity, bias=bf[:, oc:oc + 1]),
                   [("SE", 1)])
                bank_done(bank, "SE")
            w8("scalar", "SE")
            qk_stg_free[which] = dma(
                "scalar",
                sdst.rearrange("(c p) t -> p c t", p=128)
                [:, :, i * 512:(i + 1) * 512],
                stg[:, :].rearrange("p (c t) -> p c t", c=DC), "S2")
        for tj in range(4):
            tt = i * 4 + tj
            vb = tj % 2
            bank, pbank = (("pF", pF) if vb == 0 else ("pG", pG))
            if vstg_free[vb] is not None:
                w8("vector", "S2", vstg_free[vb])
            for h in range(2):
                bank_wait(bank)
                for icc in range(DC):
                    mm(pbank[:, 0:512],
                       LNxTv(icc)[:, tt * 128:(tt + 1) * 128],
                       WTv(WvT, icc)[:, h * 512:(h + 1) * 512],
                       icc == 0, icc == DC - 1, inc=(icc == DC - 1))
                w8("vector", "TE")
                op("vector", lambda e, vb=vb, h=h, pb=pbank: e.tensor_tensor(
                    vstg[vb][:, h * 512:(h + 1) * 512], pb[:, 0:512],
                    bvB[:, h * 512:(h + 1) * 512], Alu.add), [("VE", 1)])
                bank_done(bank, "VE")
            w8("scalar", "VE")
            vstg_free[vb] = dma("scalar", sV[tt * 128:(tt + 1) * 128, :],
                                vstg[vb][:, :], "S2")

    prep_attn = [(nm, ci) for nm in ("wk", "wq", "wv") for ci in range(DC)]
    wT_attn = 0
    for qr in range(4):
        for tt in range(qr * 8, qr * 8 + 8):
            ln1_tile(tt)
            if qr == 0:
                for nm, ci in prep_attn[3 * (tt % 8): 3 * (tt % 8) + 3]:
                    w_prep_chunk(nm, ci, False)
        lnxT_quarter(qr)
        if qr == 0:
            gd_all = S.now("GD")
            wT_load("wk", WkT, gd_all)
            wT_load("wq", WqT, gd_all)
            wT_attn = wT_load("wv", WvT, gd_all)
        if qr >= 1:
            qkv_t512(2 * (qr - 1), wT_attn)
            qkv_t512(2 * qr - 1, wT_attn)
    qkv_t512(6, wT_attn)
    qkv_t512(7, wT_attn)

    mA1 = marks()

    nc.sbuf_base = markEA
    KbT = sb("KbT", [128, DC * T], BF16)
    Vres = sb("Vres", [128, NT128 * D], BF16)
    PT = sb("PT", [128, NT128 * QB], BF16)
    qtl = sb("qtl", [128, DC * QB], BF16)
    atoT = sb("atoT", [128, DC * QB], BF16)
    denrow = sb("denrow", [1, QB], F32)
    denrec = sb("denrec", [1, QB], F32)
    rdenT = sb("rdenT", [128, QB], F32)
    atl = [sb(f"atl{i}", [128, DC * 128], BF16) for i in range(3)]

    def KbTv(c):
        return KbT[:, c * T:(c + 1) * T]

    def Vresv(k):
        return Vres[:, k * D:(k + 1) * D]

    def PTv(k):
        return PT[:, k * QB:(k + 1) * QB]

    qgate(mA1, ("sync", "scalar", "gpsimd"))

    for ci in range(DC):
        w_prep_chunk("wo", ci, False)
    wo_gd = S.now("GD")

    kb_ld = dma("sync", KbT[:, :].rearrange("p (c t) -> p c t", c=DC),
                sK.rearrange("(c p) t -> p c t", p=128), "LD")
    vres_ld = dma("sync", Vres[:, :].rearrange("p (k d) -> p k d", k=NT128),
                  sV.rearrange("(k p) d -> p k d", p=128), "LD")

    qt_ld = [None]
    qt_ld[0] = dma("sync", qtl[:, :].rearrange("p (c t) -> p c t", c=DC),
                   sQ.rearrange("(c p) t -> p c t", p=128)[:, :, 0:QB], "LD")

    exp_cnt = {}
    ato_st = [None] * NQ

    w8("tensor", "VE")
    w8("tensor", "SE")

    def a2_qblock(q):
        w8("tensor", "LD", qt_ld[0])
        if q == 0:
            w8("tensor", "LD", kb_ld)
        for k in range(NT128):
            bank, pbank = (("pA", pA) if k % 2 == 0 else ("pB", pB))
            bank_wait(bank)
            for icc in range(DC):
                mm(pbank[:, 0:QB],
                   KbTv(icc)[:, k * 128:(k + 1) * 128],
                   qtl[:, icc * QB:(icc + 1) * QB],
                   icc == 0, icc == DC - 1, inc=(icc == DC - 1))
            w8("scalar", "TE")
            op("scalar", lambda e, k=k, pb=pbank: e.activation(
                PTv(k)[:, :], pb[:, 0:QB], Act.Exp,
                bias=0.0, scale=mcol[:, k:k + 1]), [("SE", 1)])
            exp_cnt[k] = S.now("SE")
            bank_done(bank, "SE")
        s_te = S.now("TE")
        if q + 1 < NQ:
            w8("sync", "TE", s_te)
            qt_ld[0] = dma(
                "sync", qtl[:, :].rearrange("p (c t) -> p c t", c=DC),
                sQ.rearrange("(c p) t -> p c t", p=128)
                [:, :, (q + 1) * QB:(q + 2) * QB], "LD")
        if q == 0:
            w8("tensor", "LD", vres_ld)
        bank_wait("pD")
        for k in range(NT128):
            w8("tensor", "SE", exp_cnt[k])
            mm(pD[0:1, 0:QB], onecol[:, :], PTv(k)[:, :],
               k == 0, k == NT128 - 1, inc=(k == NT128 - 1))
        w8("vector", "TE")
        op("vector", lambda e: e.tensor_copy(denrow[:, :], pD[0:1, 0:QB]),
           [("VE", 1)])
        bank_done("pD", "VE")
        vhop()
        op("vector", lambda e: e.reciprocal(denrec[:, :], denrow[:, :]),
           [("VE", 1)])
        den_ve = S.now("VE")
        bank_wait("pM")
        w8("tensor", "VE", den_ve)
        mm(pM[:, 0:QB], ones1[:, :], denrec[:, :], True, True, inc=True)
        w8("vector", "TE")
        op("vector", lambda e: e.tensor_copy(rdenT[:, :], pM[:, 0:QB]),
           [("VE", 1)])
        vhop()
        bank_done("pM", "VE")
        pvb = (("pC", pC), ("pE", pE), ("pF", pF), ("pG", pG))
        for dcc in range(DC):
            bank, pbank = pvb[dcc % 4]
            bank_wait(bank)
            for k in range(NT128):
                mm(pbank[:, 0:QB],
                   Vresv(k)[:, dcc * 128:(dcc + 1) * 128],
                   PTv(k)[:, :],
                   k == 0, k == NT128 - 1, inc=(k == NT128 - 1))
            w8("vector", "TE")
            if dcc == 0 and q > 0:
                w8("vector", "S2", ato_st[q - 1])
            op("vector", lambda e, dcc=dcc, pb=pbank: e.tensor_tensor(
                atoT[:, dcc * QB:(dcc + 1) * QB], pb[:, 0:QB],
                rdenT[:, :], Alu.mult), [("VE", 1)])
            bank_done(bank, "VE")
        w8("scalar", "VE")
        ato_st[q] = dma(
            "scalar",
            sATO.rearrange("(c p) t -> p c t", p=128)
            [:, :, q * QB:(q + 1) * QB],
            atoT[:, :].rearrange("p (c t) -> p c t", c=DC), "S2")

    dq_chunks = ([("wup", ci, 0) for ci in range(ICN)]
                 + [("wgate", ci, 0) for ci in range(ICN)]
                 + [("wdown", ci, sec) for ci in range(DC)
                    for sec in range(3)])
    per_q = (len(dq_chunks) + NQ - 1) // NQ
    for q in range(NQ):
        a2_qblock(q)
        for nm, ci, sec in dq_chunks[q * per_q:(q + 1) * per_q]:
            w_prep_chunk(nm, ci, True, sec)
    dq_gd = S.now("GD")

    atl_ld = [None] * NT128
    for tt in range(3):
        w8("sync", "S2", ato_st[tt // 4])
        atl_ld[tt] = dma(
            "sync", atl[tt % 3][:, :].rearrange("p (c t) -> p c t", c=DC),
            sATO.rearrange("(c p) t -> p c t", p=128)
            [:, :, tt * 128:(tt + 1) * 128], "LD")

    mA2 = marks()

    nc.sbuf_base = markEA
    WoT = sb("WoT", [128, DC * D], BF16)
    boB = sb("boB", [128, D], F32)
    xr2 = [sb(f"xr2_{i}", [128, D], F32) for i in range(2)]
    xrb = [sb(f"xrb{i}", [128, D], F32) for i in range(2)]
    x1st = [sb(f"x1st{i}", [128, D], F32) for i in range(2)]

    qgate(mA2, ("sync", "scalar", "gpsimd"))
    woT_s2 = wT_load("wo", WoT, wo_gd)

    for h in range(2):
        if h:
            w8("sync", "TE")
        ld_bo = dma("sync", brow[:, :], b_h["bo"][:, h * 512:(h + 1) * 512],
                    "LD")
        w8("tensor", "LD", ld_bo)
        bank_wait("pM")
        mm(pM[:, 0:512], ones1[:, :], brow[:, :],
           True, True, inc=True)
        w8("vector", "TE")
        op("vector", lambda e, h=h: e.tensor_copy(
            boB[:, h * 512:(h + 1) * 512], pM[:, 0:512]), [("VE", 1)])
        bank_done("pM", "VE")
    boB_ve = S.now("VE")

    xr2_free = [None, None]
    x1_store = [None, None]
    xrb_ve = [None, None]
    xrb_ge = [None, None]
    atl_te = [None] * NT128

    for tt in range(NT128):
        buf = tt % 2
        if xr2_free[buf] is not None:
            w8("gpsimd", "GE", xr2_free[buf])
        xld = dma("gpsimd", xr2[buf][:, :], x_h[tt * 128:(tt + 1) * 128, :],
                  "GD")
        w8("gpsimd", "GD", xld)
        w8("gpsimd", "VE", xrb_ve[buf] if xrb_ve[buf] is not None
           else boB_ve)
        op("gpsimd", lambda e, b=buf: e.tensor_tensor(
            xrb[b][:, :], xr2[b][:, :], boB[:, :], Alu.add), [("GE", 1)])
        xr2_free[buf] = xrb_ge[buf] = S.now("GE")
        if DEBUG_SERIAL:
            for sem in ("LD", "S2", "GE", "VE", "SE", "TE"):
                for eng in ("tensor", "vector", "scalar", "gpsimd", "sync"):
                    w8(eng, sem)
        w8("tensor", "LD", atl_ld[tt])
        if tt == 0:
            w8("tensor", "S2", woT_s2)
        for h in range(2):
            bank, pbank = (("pF", pF) if h == 0 else ("pG", pG))
            bank_wait(bank)
            for icc in range(DC):
                mm(pbank[:, 0:512],
                   atl[tt % 3][:, icc * 128:(icc + 1) * 128],
                   WTv(WoT, icc)[:, h * 512:(h + 1) * 512],
                   icc == 0, icc == DC - 1, inc=(icc == DC - 1))
            w8("vector", "TE")
            if h == 0:
                w8("vector", "GE", xrb_ge[buf])
                if x1_store[buf] is not None:
                    w8("vector", "S2", x1_store[buf])
            op("vector", lambda e, h=h, pb=pbank, b=buf: e.tensor_tensor(
                x1st[b][:, h * 512:(h + 1) * 512], pb[:, 0:512],
                xrb[b][:, h * 512:(h + 1) * 512], Alu.add), [("VE", 1)])
            bank_done(bank, "VE")
        atl_te[tt] = S.now("TE")
        xrb_ve[buf] = S.now("VE")
        if tt + 3 < NT128:
            w8("sync", "TE", atl_te[tt])
            w8("sync", "S2", ato_st[(tt + 3) // 4])
            atl_ld[tt + 3] = dma(
                "sync",
                atl[(tt + 3) % 3][:, :].rearrange("p (c t) -> p c t", c=DC),
                sATO.rearrange("(c p) t -> p c t", p=128)
                [:, :, (tt + 3) * 128:(tt + 4) * 128], "LD")
        op("vector", lambda e, b=buf: e.bn_stats(lnst[:, 0:6],
                                                 x1st[b][:, 0:512]))
        op("vector", lambda e, b=buf: e.bn_stats(lnst[:, 6:12],
                                                 x1st[b][:, 512:1024]))
        op("vector", lambda e, tt=tt: e.bn_aggr(
            statMV4[:, 4 * tt:4 * tt + 2], lnst[:, 0:6]))
        op("vector", lambda e, tt=tt: e.bn_aggr(
            statMV4[:, 4 * tt + 2:4 * tt + 4], lnst[:, 6:12]))
        op("vector", lambda e, tt=tt, b=buf: e.tensor_reduce(
            maxC[:, tt:tt + 1], x1st[b][:, :], AX, Alu.max))
        op("vector", lambda e, tt=tt, b=buf: e.tensor_reduce(
            minC[:, tt:tt + 1], x1st[b][:, :], AX, Alu.min), [("VE", 1)])
        ve_now = S.now("VE")
        w8("scalar", "VE", ve_now)
        x1_store[buf] = dma("scalar", sX1[tt * 128:(tt + 1) * 128, :],
                            x1st[buf][:, :], "S2")

    mA2 = marks()

    w8("vector", "GE")
    s4 = statMV4[:, :].rearrange("p (t f) -> p t f", f=4)
    mAv, vAv, mBv, vBv = (s4[:, :, i:i + 1] for i in range(4))
    b3 = bscr[:, :].rearrange("p (k t) -> p k t", k=3)
    t0v, t1v, t2v = (b3[:, i, :] for i in range(3))
    op("vector", lambda e: e.tensor_tensor(
        t0v.rearrange("p (t o) -> p t o", o=1), mAv, mBv, Alu.add))
    op("vector", lambda e: e.tensor_tensor(
        t1v.rearrange("p (t o) -> p t o", o=1), mAv, mBv, Alu.subtract))
    op("vector", lambda e: e.tensor_tensor(
        t2v.rearrange("p (t o) -> p t o", o=1), vAv, vBv, Alu.add))
    vhop()
    op("vector", lambda e: e.tensor_scalar(statM2[:, :], t0v, 0.5, None,
                                           Alu.mult))
    op("vector", lambda e: e.tensor_tensor(t0v, t1v, t1v, Alu.mult))
    op("vector", lambda e: e.tensor_scalar(statV2[:, :], t2v, 0.5, None,
                                           Alu.mult))
    vhop()
    op("vector", lambda e: e.scalar_tensor_tensor(
        statV2[:, :], t0v, 0.25, statV2[:, :], Alu.mult, Alu.add),
       [("VE", 1)])
    w8("scalar", "VE")
    op("scalar", lambda e: e.activation(statR[:, :], statV2[:, :],
                                        Act.Sqrt, bias=cst[:, 3:4]),
       [("SE", 1)])
    w8("vector", "SE")
    op("vector", lambda e: e.reciprocal(statR[:, :], statR[:, :]))
    vhop()
    op("vector", lambda e: e.tensor_tensor(t0v, maxC[:, :], statM2[:, :],
                                           Alu.subtract))
    op("vector", lambda e: e.tensor_tensor(t1v, statM2[:, :], minC[:, :],
                                           Alu.subtract))
    vhop()
    op("vector", lambda e: e.tensor_tensor(t2v, t0v, t1v, Alu.max))
    vhop()
    op("vector", lambda e: e.tensor_tensor(t2v, t2v, statR[:, :], Alu.mult))
    vhop()
    op("vector", lambda e: e.tensor_reduce(smax[:, 0:1], t2v, AX,
                                           Alu.max), [("VE", 1)])
    w8("gpsimd", "VE")
    dma("gpsimd", ccI[1].ap(), smax[:, 0:1], "GD")
    w8("gpsimd", "GD")
    collective(1)
    w8("gpsimd", "CC")
    g = dma("gpsimd", armaxs[:, 0:1], ccO[1].ap(), "GD")
    w8("gpsimd", "GD", g)
    cross_part_max(armaxs[:, 0:1], sc1[:, 0:1])
    op("vector", lambda e: e.tensor_scalar(sc1[:, 1:2], sc1[:, 0:1],
                                           1.0 / 32767.0, 1e-12,
                                           Alu.mult, Alu.max))
    vhop()
    op("vector", lambda e: e.reciprocal(sc1[:, 2:3], sc1[:, 1:2]),
       [("VE", 1)])
    vhop()
    bcast(sc1[:, 1:3], scB[:, 0:2])
    vhop()
    op("vector", lambda e: e.tensor_scalar(aB[:, :], statR[:, :],
                                           scB[:, 1:2], None, Alu.mult))
    vhop()
    op("vector", lambda e: e.tensor_tensor(bBv[:, :], statM2[:, :],
                                           aB[:, :], Alu.mult),
       [("VE", 1)])
    vhop()

    nc.sbuf_base = markP
    xqT = sb("xqT", [128, DC * T], BF16)
    WuT = sb("WuT", [128, DC * DI], BF16)
    WgT = sb("WgT", [128, DC * DI], BF16)
    slab = [sb(f"slab{i}", [128, 2048], F32) for i in range(2)]
    x1r = [sb(f"x1r{i}", [128, D], F32) for i in range(2)]
    xqi = sb("xqi", [128, D], I32)
    xqs = [sb(f"xqs{i}", [128, D], BF16) for i in range(2)]

    def xqTv(c):
        return xqT[:, c * T:(c + 1) * T]

    def WiTv(t, c):
        return t[:, c * DI:(c + 1) * DI]

    qgate(mA2, ("sync", "scalar", "gpsimd"))
    wuT_s2 = wT_load("wup", WuT, dq_gd)

    xq_done = [None] * NT128
    x1r_free = [None, None]
    xq_ve = [None] * NT128
    xqT_ready = [0] * 4

    for qr in range(4):
        for tt in range(qr * 8, qr * 8 + 8):
            buf = tt % 2
            if x1r_free[buf] is not None:
                w8("sync", "VE", x1r_free[buf])
            ld = dma("sync", x1r[buf][:, :],
                     sX1[tt * 128:(tt + 1) * 128, :], "LD")
            w8("vector", "LD", ld)
            if tt >= 2:
                w8("vector", "LD", xq_done[tt - 2])
            op("vector", lambda e, b=buf, tt=tt: e.tensor_scalar(
                xqi[:, :], x1r[b][:, :], aB[:, tt:tt + 1],
                bBv[:, tt:tt + 1], Alu.mult, Alu.subtract))
            op("vector", lambda e, b=buf: e.tensor_scalar(
                xqs[b][:, :], xqi[:, :], scB[:, 0:1], None, Alu.mult),
               [("VE", 1)])
            x1r_free[buf] = S.now("VE")
            w8("sync", "VE")
            xq_done[tt] = dma("sync", sXQ[tt * 128:(tt + 1) * 128, :],
                              xqs[buf][:, :], "LD")
        w8("scalar", "LD", xq_done[qr * 8 + 7])
        for c in range(DC):
            dmaT("scalar", xqTv(c)[:, qr * 1024:(qr + 1) * 1024],
                 sXQ[qr * 1024:(qr + 1) * 1024, c * 128:(c + 1) * 128],
                 "S2")
        xqT_ready[qr] = S.now("S2")

    wgT_s2 = wT_load("wgate", WgT, dq_gd)

    slab_free = [None, None]

    def f1_pass(wT, sdst, rcol, wT_s2):
        w8("tensor", "S2", wT_s2)
        for icc in range(ICN):
            for hh in range(2):
                buf = hh
                if slab_free[buf] is not None:
                    for semnm, cv in slab_free[buf]:
                        w8("scalar", semnm, cv)
                for ii in range(4):
                    i = hh * 4 + ii
                    if icc == 0:
                        w8("tensor", "S2", xqT_ready[i // 2])
                    bank, pbank = (("pA", pA) if i % 2 == 0 else ("pB", pB))
                    bank_wait(bank)
                    for h in range(DC):
                        mm(pbank[:, 0:512],
                           WiTv(wT, h)[:, icc * 128:(icc + 1) * 128],
                           xqTv(h)[:, i * 512:(i + 1) * 512],
                           h == 0, h == DC - 1, inc=(h == DC - 1))
                    w8("scalar", "TE")
                    op("scalar", lambda e, b=buf, ii=ii, pb=pbank: e.copy(
                        slab[b][:, ii * 512:(ii + 1) * 512], pb[:, 0:512]),
                       [("SE", 1)])
                    bank_done(bank, "SE")
                w8("vector", "SE")
                op("vector", lambda e, b=buf, icc=icc, hh=hh, rc=rcol:
                   e.tensor_reduce(rc[:, 2 * icc + hh:2 * icc + hh + 1],
                                   slab[b][:, :], AX, Alu.max,
                                   apply_absolute_value=True), [("VE", 1)])
                w8("scalar", "SE")
                st = dma("scalar",
                         sdst.rearrange("(c p) t -> p c t", p=128)
                         [:, icc:icc + 1, hh * 2048:(hh + 1) * 2048],
                         slab[buf][:, :].rearrange("p (c t) -> p c t", c=1),
                         "S2")
                slab_free[buf] = [("VE", S.now("VE")), ("S2", st)]

    f1_pass(WuT, sUP, rupC, wuT_s2)
    f1_pass(WgT, sGA, rgaC, wgT_s2)
    mF1 = marks()

    op("vector", lambda e: e.tensor_reduce(smax[:, 0:1], rupC[:, :], AX,
                                           Alu.max))
    op("vector", lambda e: e.tensor_reduce(smax[:, 1:2], rgaC[:, :], AX,
                                           Alu.max), [("VE", 1)])
    w8("gpsimd", "VE")
    dma("gpsimd", ccI[2].ap(), smax[:, 0:2], "GD")
    w8("gpsimd", "GD")
    collective(2)
    w8("gpsimd", "CC")
    g = dma("gpsimd", armaxs[:, 0:2], ccO[2].ap(), "GD")
    w8("gpsimd", "GD", g)
    cross_part_max(armaxs[:, 0:1], sc1[:, 3:4])
    cross_part_max(armaxs[:, 1:2], sc1[:, 6:7])
    op("vector", lambda e: e.tensor_scalar(sc1[:, 4:5], sc1[:, 3:4],
                                           1.0 / 32767.0, 1e-12,
                                           Alu.mult, Alu.max))
    op("vector", lambda e: e.tensor_scalar(sc1[:, 7:8], sc1[:, 6:7],
                                           1.0 / 32767.0, 1e-12,
                                           Alu.mult, Alu.max))
    vhop()
    op("vector", lambda e: e.reciprocal(sc1[:, 5:6], sc1[:, 4:5]))
    vhop()
    op("vector", lambda e: e.reciprocal(sc1[:, 8:9], sc1[:, 7:8]),
       [("VE", 1)])
    vhop()
    bcast(sc1[:, 5:6], scB[:, 2:3])
    bcast(sc1[:, 7:9], scB[:, 4:6])
    scB_ve_b2 = S.now("VE")

    nc.sbuf_base = markP
    NH = 2048
    gar = [sb(f"gar{i}", [128, NH], F32) for i in range(2)]
    upr = [sb(f"upr{i}", [128, NH], F32) for i in range(4)]
    tg = [sb(f"tg{i}", [128, NH], I32) for i in range(3)]
    sg_ = [sb(f"sg_{i}", [128, NH], F32) for i in range(2)]
    ts_ = [sb(f"ts_{i}", [128, NH], I32) for i in range(2)]
    th = [sb(f"th{i}", [128, NH], I32) for i in range(3)]
    tg2 = [sb(f"tg2{i}", [128, NH], I32) for i in range(2)]
    tup = [sb(f"tup{i}", [128, NH], I32) for i in range(2)]
    ott = [sb(f"ott{i}", [128, NH], F32) for i in range(2)]

    qgate(mF1, ("sync", "gpsimd"))

    UN = 2 * ICN
    guards = {}

    def guard_w(eng, key):
        for semnm, cv in guards.get(key, ()):
            w8(eng, semnm, cv)

    def gslab(icc, hh):
        return (sGA.rearrange("(c p) t -> p c t", p=128)
                [:, icc:icc + 1, hh * NH:(hh + 1) * NH])

    def uslab(icc, hh):
        return (sUP.rearrange("(c p) t -> p c t", p=128)
                [:, icc:icc + 1, hh * NH:(hh + 1) * NH])

    def oslab(icc, hh):
        return (sOT.rearrange("(c p) t -> p c t", p=128)
                [:, icc:icc + 1, hh * NH:(hh + 1) * NH])

    r1 = lambda t: t[:, :].rearrange("p (c t) -> p c t", c=1)

    def stage_gload(u):
        b2 = u % 2
        guard_w("sync", ("gar", b2))
        lds["g", u] = dma("sync", r1(gar[b2]), gslab(u // 2, u % 2), "LD")

    def stage_uload(u):
        b4 = u % 4
        guard_w("gpsimd", ("upr", b4))
        lds["u", u] = dma("gpsimd", r1(upr[b4]), uslab(u // 2, u % 2), "GD")

    def stage_s1(u):
        w8("scalar", "LD", lds["g", u])
        guard_w("scalar", ("tg", u % 3))
        op("scalar", lambda e, b=u % 2, o=u % 3: e.activation(
            tg[o][:, :], gar[b][:, :], Act.Identity, bias=0.0,
            scale=scB[:, 5:6]), [("SE", 1)])
        guard_w("scalar", ("sg_", u % 2))
        op("scalar", lambda e, o3=u % 3, o=u % 2: e.activation(
            sg_[o][:, :], tg[o3][:, :], Act.Sigmoid, bias=0.0,
            scale=scB[:, 4:5]), [("SE", 1)])
        sems_u["sig", u] = S.now("SE")
        guards[("gar", u % 2)] = [("SE", sems_u["sig", u])]

    def stage_v1(u):
        w8("vector", "SE", sems_u["sig", u])
        guard_w("vector", ("ts_", u % 2))
        op("vector", lambda e, i=u % 2, o=u % 2: e.tensor_scalar(
            ts_[o][:, :], sg_[i][:, :], 65536.0, 65535.0,
            Alu.mult, Alu.min), [("VE", 1)])
        sems_u["ts", u] = S.now("VE")
        guards[("sg_", u % 2)] = [("VE", sems_u["ts", u])]

    def stage_g1(u):
        w8("gpsimd", "VE", sems_u["ts", u])
        guard_w("gpsimd", ("th", u % 3))
        op("gpsimd", lambda e, i=u % 2, i3=u % 3, o=u % 3: e.tensor_tensor(
            th[o][:, :], ts_[i][:, :], tg[i3][:, :], Alu.mult), [("GE", 1)])
        sems_u["th", u] = S.now("GE")
        guards[("ts_", u % 2)] = [("GE", sems_u["th", u])]
        guards[("tg", u % 3)] = [("GE", sems_u["th", u])]

    def stage_f2red(u):
        w8("vector", "GE", sems_u["th", u])
        op("vector", lambda e, o=u % 3, u=u: e.tensor_reduce(
            rthC[:, u:u + 1], th[o][:, :], AX, Alu.max,
            apply_absolute_value=True), [("VE", 1)])
        guards[("th", u % 3)] = [("VE", S.now("VE"))]

    def stage_s2(u):
        w8("scalar", "GE", sems_u["th", u])
        w8("scalar", "GD", lds["u", u])
        guard_w("scalar", ("tg2", u % 2))
        op("scalar", lambda e, i=u % 3, o=u % 2: e.activation(
            tg2[o][:, :], th[i][:, :], Act.Identity, bias=0.0,
            scale=scB[:, 6:7]), [("SE", 1)])
        guards[("th", u % 3)] = [("SE", S.now("SE"))]
        guard_w("scalar", ("tup", u % 2))
        op("scalar", lambda e, b=u % 4, o=u % 2: e.activation(
            tup[o][:, :], upr[b][:, :], Act.Identity, bias=0.0,
            scale=scB[:, 2:3]), [("SE", 1)])
        sems_u["tup", u] = S.now("SE")
        guards[("upr", u % 4)] = [("SE", sems_u["tup", u])]

    def stage_v2(u):
        w8("vector", "SE", sems_u["tup", u])
        guard_w("vector", ("ott", u % 2))
        op("vector", lambda e, o=u % 2: e.tensor_tensor(
            ott[o][:, :], tg2[o][:, :], tup[o][:, :], Alu.mult))
        op("vector", lambda e, o=u % 2, u=u: e.tensor_reduce(
            rotC[:, u:u + 1], ott[o][:, :], AX, Alu.max,
            apply_absolute_value=True), [("VE", 1)])
        sems_u["ot", u] = S.now("VE")
        guards[("tg2", u % 2)] = [("VE", sems_u["ot", u])]
        guards[("tup", u % 2)] = [("VE", sems_u["ot", u])]

    def stage_store(u):
        w8("scalar", "VE", sems_u["ot", u])
        st = dma("scalar", oslab(u // 2, u % 2), r1(ott[u % 2]), "S2")
        guards[("ott", u % 2)] = [("VE", sems_u["ot", u]), ("S2", st)]

    lds = {}
    sems_u = {}

    w8("scalar", "VE", scB_ve_b2)
    for i in range(UN + 4):
        if i < UN:
            stage_gload(i)
        if 0 <= i - 1 < UN:
            stage_s1(i - 1)
        if 0 <= i - 2 < UN:
            stage_v1(i - 2)
        if 0 <= i - 3 < UN:
            stage_g1(i - 3)
        if 0 <= i - 4 < UN:
            stage_f2red(i - 4)

    vhop()
    op("vector", lambda e: e.tensor_reduce(smax[:, 0:1], rthC[:, :], AX,
                                           Alu.max), [("VE", 1)])
    w8("gpsimd", "VE")
    dma("gpsimd", ccI[3].ap(), smax[:, 0:1], "GD")
    w8("gpsimd", "GD")
    collective(3)
    w8("gpsimd", "CC")
    g = dma("gpsimd", armaxs[:, 0:1], ccO[3].ap(), "GD")
    w8("gpsimd", "GD", g)
    cross_part_max(armaxs[:, 0:1], sc1[:, 9:10])
    op("vector", lambda e: e.tensor_scalar(sc1[:, 10:11], sc1[:, 7:8],
                                           1.0 / 65536.0, None, Alu.mult))
    vhop()
    op("vector", lambda e: e.tensor_tensor(sc1[:, 11:12], sc1[:, 10:11],
                                           sc1[:, 9:10], Alu.mult))
    vhop()
    op("vector", lambda e: e.tensor_scalar(sc1[:, 12:13], sc1[:, 11:12],
                                           1.0 / 32767.0, 1e-12,
                                           Alu.mult, Alu.max))
    vhop()
    op("vector", lambda e: e.reciprocal(sc1[:, 13:14], sc1[:, 12:13]))
    vhop()
    op("vector", lambda e: e.tensor_tensor(sc1[:, 14:15], sc1[:, 10:11],
                                           sc1[:, 13:14], Alu.mult),
       [("VE", 1)])
    vhop()
    bcast(sc1[:, 14:15], scB[:, 6:7])
    b3_ve = S.now("VE")
    w8("scalar", "VE", b3_ve)

    for i in range(UN + 6):
        if i < UN:
            stage_gload(i)
        if 0 <= i - 1 < UN:
            stage_s1(i - 1)
        if 0 <= i - 2 < UN:
            stage_uload(i - 2)
            stage_v1(i - 2)
        if 0 <= i - 3 < UN:
            stage_g1(i - 3)
        if 0 <= i - 5 < UN:
            stage_s2(i - 5)
        if 0 <= i - 6 < UN:
            stage_v2(i - 6)
            stage_store(i - 6)

    mF3 = marks()

    vhop()
    op("vector", lambda e: e.tensor_reduce(smax[:, 0:1], rotC[:, :], AX,
                                           Alu.max), [("VE", 1)])
    w8("gpsimd", "VE")
    dma("gpsimd", ccI[4].ap(), smax[:, 0:1], "GD")
    w8("gpsimd", "GD")
    collective(4)
    w8("gpsimd", "CC")
    g = dma("gpsimd", armaxs[:, 0:1], ccO[4].ap(), "GD")
    w8("gpsimd", "GD", g)
    cross_part_max(armaxs[:, 0:1], sc1[:, 15:16])
    op("vector", lambda e: e.tensor_tensor(sc1[:, 16:17], sc1[:, 12:13],
                                           sc1[:, 4:5], Alu.mult))
    vhop()
    op("vector", lambda e: e.tensor_tensor(sc1[:, 17:18], sc1[:, 16:17],
                                           sc1[:, 15:16], Alu.mult))
    vhop()
    op("vector", lambda e: e.tensor_scalar(sc1[:, 18:19], sc1[:, 17:18],
                                           1.0 / 32767.0, 1e-12,
                                           Alu.mult, Alu.max))
    vhop()
    op("vector", lambda e: e.reciprocal(sc1[:, 19:20], sc1[:, 18:19]))
    vhop()
    op("vector", lambda e: e.tensor_tensor(sc1[:, 20:21], sc1[:, 16:17],
                                           sc1[:, 19:20], Alu.mult),
       [("VE", 1)])
    vhop()
    bcast(sc1[:, 18:19], scB[:, 7:8])
    bcast(sc1[:, 20:21], scB[:, 8:9])
    scB_ve_b4 = S.now("VE")

    nc.sbuf_base = markP
    WdT = sb("WdT", [128, ICN * D], BF16)
    otl = [sb(f"otl{i}", [128, ICN * 256], F32) for i in range(2)]
    tol = [sb(f"tol{i}", [128, ICN * 128], F32) for i in range(2)]
    oql = [sb(f"oql{i}", [128, ICN * 128], BF16) for i in range(2)]
    x1f = [sb(f"x1f{i}", [128, D], F32) for i in range(4)]
    outst = [sb(f"outst{i}", [128, D], F32) for i in range(2)]

    qgate(mF3, ("sync", "scalar", "gpsimd"))
    wdT_s2 = wT_load("wdown", WdT, dq_gd)

    f4_se = [None, None]
    f4_ve = [None, None, None, None]
    oq_se_l = [None, None]
    mm_te = [None, None]
    out_st = [None, None]
    f4_lds = {}

    def f4_load(pp):
        buf = pp % 2
        if f4_se[buf] is not None:
            w8("sync", "SE", f4_se[buf])
        f4_lds["o", pp] = dma(
            "sync", otl[buf][:, :].rearrange("p (c t) -> p c t", c=ICN),
            sOT.rearrange("(c p) t -> p c t", p=128)
            [:, :, pp * 256:(pp + 1) * 256], "LD")
        for j in range(2):
            tt = 2 * pp + j
            if f4_ve[tt % 4] is not None:
                w8("sync", "VE", f4_ve[tt % 4])
            f4_lds["x", tt] = dma("sync", x1f[tt % 4][:, :],
                                  sX1[tt * 128:(tt + 1) * 128, :], "LD")

    def f4_scal(tt):
        pp, j = tt // 2, tt % 2
        buf = pp % 2
        ob = tt % 2
        w8("scalar", "LD", f4_lds["o", pp])
        if tt == 0:
            w8("scalar", "VE", scB_ve_b4)
        ot3 = (otl[buf][:, :].rearrange("p (c t) -> p c t", c=ICN)
               [:, :, j * 128:(j + 1) * 128])
        op("scalar", lambda e, b=buf, a=ot3, ob=ob: e.activation(
            tol[ob][:, :], a, Act.Identity, bias=0.0, scale=scB[:, 8:9]))
        if mm_te[ob] is not None:
            w8("scalar", "TE", mm_te[ob])
        op("scalar", lambda e, ob=ob: e.activation(
            oql[ob][:, :], tol[ob][:, :], Act.Identity, bias=0.0,
            scale=scB[:, 7:8]), [("SE", 1)])
        oq_se_l[ob] = S.now("SE")
        if j == 1:
            f4_se[buf] = S.now("SE")

    def f4_mm(tt):
        ob = tt % 2
        if tt == 0:
            w8("tensor", "S2", wdT_s2)
        w8("tensor", "SE", oq_se_l[ob])
        for h in range(2):
            bank, pbank = (("pA", pA) if h == 0 else ("pB", pB))
            bank_wait(bank)
            for icc in range(ICN):
                mm(pbank[:, 0:512],
                   oql[ob][:, icc * 128:(icc + 1) * 128],
                   WdT[:, icc * D + h * 512:icc * D + (h + 1) * 512],
                   icc == 0, icc == ICN - 1, inc=(icc == ICN - 1))
            w8("vector", "TE")
            if h == 0:
                w8("vector", "LD", f4_lds["x", tt])
                if out_st[ob] is not None:
                    w8("vector", "S2", out_st[ob])
            op("vector", lambda e, b=ob, x4=tt % 4, h=h, pb=pbank:
               e.tensor_tensor(
                   outst[b][:, h * 512:(h + 1) * 512], pb[:, 0:512],
                   x1f[x4][:, h * 512:(h + 1) * 512], Alu.add), [("VE", 1)])
            bank_done(bank, "VE")
        mm_te[ob] = S.now("TE")
        f4_ve[tt % 4] = S.now("VE")
        w8("scalar", "VE")
        out_st[ob] = dma("scalar", out_h[tt * 128:(tt + 1) * 128, :],
                         outst[ob][:, :], "S2")

    for t in range(NT128 + 2):
        if t % 2 == 0 and t // 2 < NT128 // 2:
            f4_load(t // 2)
        if 0 <= t - 1 < NT128:
            f4_scal(t - 1)
        if 0 <= t - 2 < NT128:
            f4_mm(t - 2)

    w8("sync", "S2")
    w8("gpsimd", "S2")

    with nc.Block() as block:
        def runner(name):
            def body(eng):
                for e, kind, payload in S.items:
                    if e != name:
                        continue
                    if kind == "wait":
                        semnm, val = payload
                        eng.wait_ge(sems[semnm], val)
                    else:
                        fn, incs = payload
                        inst = fn(eng)
                        for semnm, n in incs:
                            inst = inst.then_inc(sems[semnm], n)
            return body
        block.sync(runner("sync"))
        block.scalar(runner("scalar"))
        block.tensor(runner("tensor"))
        block.vector(runner("vector"))
        block.gpsimd(runner("gpsimd"))

    return nc


_NC_CACHE = {}


def kernel(L0_attn_x, L0_attn_Wq, L0_attn_bq, L0_attn_Wk, L0_attn_bk,
           L0_attn_Wv, L0_attn_bv, L0_attn_Wo, L0_attn_bo,
           L0_ffn_w_gate, L0_ffn_w_up, L0_ffn_w_down):
    if "nc" not in _NC_CACHE:
        _NC_CACHE["nc"] = build()
    nc = _NC_CACHE["nc"]
    f32 = np.float32

    def cvt(a):
        return np.ascontiguousarray(np.asarray(a, dtype=f32))

    X = cvt(L0_attn_x)
    base = {
        "wq": cvt(L0_attn_Wq), "wk": cvt(L0_attn_Wk), "wv": cvt(L0_attn_Wv),
        "wo": cvt(L0_attn_Wo), "wgate": cvt(L0_ffn_w_gate),
        "wup": cvt(L0_ffn_w_up), "wdown": cvt(L0_ffn_w_down),
        "bq": cvt(L0_attn_bq).reshape(1, D),
        "bk": cvt(L0_attn_bk).reshape(1, D),
        "bv": cvt(L0_attn_bv).reshape(1, D),
        "bo": cvt(L0_attn_bo).reshape(1, D),
    }
    in_maps = [dict(base, x=np.ascontiguousarray(X[c])) for c in range(8)]
    res = run_bass_kernel_spmd(nc, in_maps, core_ids=list(range(8)))
    out = np.stack([res.results[c]["out"] for c in range(8)], axis=0)
    return out.astype(np.float32)


# version 33
# speedup vs baseline: 1.0098x; 1.0098x over previous
import contextlib

import numpy as np

import concourse.bass as bass
import concourse.mybir as mybir
from concourse.bass_utils import run_bass_kernel_spmd

F32 = mybir.dt.float32
BF16 = mybir.dt.bfloat16
I32 = mybir.dt.int32
Act = mybir.ActivationFunctionType
Alu = mybir.AluOpType
AX = mybir.AxisListType.X

T = 4096
D = 1024
DI = 3072
DC = D // 128
ICN = DI // 128
QB = 512
NQ = T // QB
NT128 = T // 128
NT512 = T // 512

DEBUG_TAPS = False
DEBUG_SERIAL = False


class Sched:

    def __init__(self):
        self.items = []
        self.cnt = {}

    def reg(self, name):
        self.cnt[name] = 0

    def do(self, eng, fn, incs=()):
        for s, n in incs:
            self.cnt[s] += n
        self.items.append((eng, "op", (fn, tuple(incs))))

    def wait(self, eng, sem, val=None):
        if val is None:
            val = self.cnt[sem]
        assert val <= self.cnt[sem], (sem, val, self.cnt[sem])
        if val > 0:
            self.items.append((eng, "wait", (sem, val)))

    def now(self, sem):
        return self.cnt[sem]


def rep_ap(col_ap, n_outer, n_rep):
    return bass.AP(col_ap.tensor, col_ap.offset,
                   [col_ap.ap[0], [1, n_outer], [0, n_rep]])


def build():
    nc = bass.Bass("TRN2", target_bir_lowering=False,
                   detect_race_conditions=False)
    S = Sched()
    for nm in ("LD", "S2", "GD", "CC", "TE", "VE", "SE", "GE"):
        S.reg(nm)

    def dram(name, shape, dtype):
        kind = "ExternalOutput" if DEBUG_TAPS else "Internal"
        return nc.dram_tensor(name, shape, dtype, kind=kind)

    x_h = nc.dram_tensor("x", [T, D], F32, kind="ExternalInput").ap()
    w_h = {}
    for nm, shp in (("wq", [D, D]), ("wk", [D, D]), ("wv", [D, D]),
                    ("wo", [D, D]), ("wgate", [DI, D]), ("wup", [DI, D]),
                    ("wdown", [D, DI])):
        w_h[nm] = nc.dram_tensor(nm, shp, F32, kind="ExternalInput").ap()
    b_h = {nm: nc.dram_tensor(nm, [1, D], F32, kind="ExternalInput").ap()
           for nm in ("bq", "bk", "bv", "bo")}
    out_h = nc.dram_tensor("out", [T, D], F32, kind="ExternalOutput").ap()

    sLNX = dram("sLNX", [T, D], BF16).ap()
    sQ = dram("sQ", [D, T], BF16).ap()
    sK = dram("sK", [D, T], BF16).ap()
    sV = dram("sV", [T, D], BF16).ap()
    sX1 = dram("sX1", [T, D], F32).ap()
    sXQ = dram("sXQ", [T, D], BF16).ap()
    sATO = dram("sATO", [D, T], BF16).ap()
    sUP = dram("sUP", [DI, T], F32).ap()
    sGA = dram("sGA", [DI, T], F32).ap()
    sOT = dram("sOT", [DI, T], F32).ap()
    sWB = {nm: dram("sWB" + nm, list(w_h[nm].shape), BF16).ap()
           for nm in w_h}
    p2f = nc.dram_tensor("p2f", [1, 128], F32).ap()
    ccI = {k: nc.dram_tensor(f"cc{k}i", [128, c], F32)
           for k, c in ((1, 1), (2, 2), (3, 1), (4, 1))}
    ccO = {k: nc.dram_tensor(f"cc{k}o", [128, c], F32)
           for k, c in ((1, 1), (2, 2), (3, 1), (4, 1))}

    stack = contextlib.ExitStack()
    nc._keepalive_stack = stack

    def sb(name, shape, dtype):
        return stack.enter_context(nc.sbuf_tensor(name, shape, dtype))

    def psum(name):
        return stack.enter_context(nc.psum_tensor(name, [128, 512], F32))

    cst = sb("cst", [128, 4], F32)
    ones1 = sb("ones1", [1, 128], F32)
    id1 = sb("id1", [1, 1], F32)
    onecol = sb("onecol", [128, 1], BF16)
    bqF = sb("bqF", [128, DC], F32)
    bkF = sb("bkF", [128, DC], F32)
    mcol = sb("mcol", [128, NT128], F32)
    statMV4 = sb("statMV4", [128, 4 * NT128], F32)
    maxC = sb("maxC", [128, NT128], F32)
    minC = sb("minC", [128, NT128], F32)
    statM2 = sb("statM2", [128, NT128], F32)
    statV2 = sb("statV2", [128, NT128], F32)
    bscr = sb("bscr", [128, 3 * NT128], F32)
    statR = sb("statR", [128, NT128], F32)
    aB = sb("aB", [128, NT128], F32)
    bBv = sb("bBv", [128, NT128], F32)
    rmaxC = sb("rmaxC", [128, NT128], F32)
    rupC = sb("rupC", [128, 2 * ICN], F32)
    rgaC = sb("rgaC", [128, 2 * ICN], F32)
    rthC = sb("rthC", [128, 2 * ICN], F32)
    rotC = sb("rotC", [128, 2 * ICN], F32)
    smax = sb("smax", [128, 2], F32)
    armaxs = sb("armaxs", [128, 2], F32)
    row1 = sb("row1", [1, 128], F32)
    sc1 = sb("sc1", [1, 24], F32)
    scB = sb("scB", [128, 10], F32)
    lnst = sb("lnst", [128, 12], F32)
    lnag = sb("lnag", [128, 4], F32)
    lnsc = sb("lnsc", [128, 8], F32)
    lntmp = sb("lntmp", [128, 2], F32)

    pA = psum("pA")
    pB = psum("pB")
    pD = psum("pD")
    pC = psum("pC")
    pE = psum("pE")
    pF = psum("pF")
    pG = psum("pG")
    pM = psum("pM")

    sems = {nm: stack.enter_context(nc.semaphore(nm))
            for nm in ("LD", "S2", "GD", "CC", "TE", "VE", "SE", "GE")}

    def op(eng, fn, incs=()):
        S.do(eng, fn, incs)

    def w8(eng, sem, val=None):
        S.wait(eng, sem, val)

    def vhop():
        op("vector", lambda e: e.engine_nop(), [("VE", 1)])
        w8("vector", "VE")

    def dma(q, out, in_, sem, slow=False):
        if slow:
            op(q, lambda e, o=out, i=in_: e.dma_start(
                out=o, in_=i, allow_slow_non_contiguous=True), [(sem, 16)])
        else:
            op(q, lambda e, o=out, i=in_: e.dma_start(out=o, in_=i),
               [(sem, 16)])
        return S.now(sem)

    def dmaT(q, out, in_, sem):
        op(q, lambda e, o=out, i=in_: e.dma_start_transpose(o, i),
           [(sem, 16)])
        return S.now(sem)

    def mm(out, lhsT, rhs, start, stop, inc=False):
        incs = [("TE", 1)] if inc else ()
        op("tensor", lambda e, o=out, l=lhsT, r=rhs, st=start, sp=stop:
           e.matmul(o, l, r, start=st, stop=sp), incs)
        return S.now("TE")

    bankg = {}

    def bank_wait(bank):
        g = bankg.get(bank)
        if g is not None:
            w8("tensor", g[0], g[1])

    def bank_done(bank, sem):
        bankg[bank] = (sem, S.now(sem))

    def bcast(src_cols, dst_tile):
        n = src_cols.shape[-1]
        bank_wait("pM")
        w8("tensor", "VE")
        mm(pM[:, 0:n], ones1[:, :], src_cols, True, True, inc=True)
        w8("vector", "TE")
        op("vector", lambda e, d=dst_tile, n=n: e.tensor_copy(d, pM[:, 0:n]),
           [("VE", 1)])
        bank_done("pM", "VE")

    def cross_part_max(src_col, dst_col):
        dma("gpsimd", p2f.rearrange("a b -> b a"), src_col, "GD")
        w8("gpsimd", "GD")
        dma("gpsimd", row1[:, :], p2f, "GD")
        w8("vector", "GD")
        op("vector", lambda e, d=dst_col: e.tensor_reduce(
            d, row1[:, :], AX, Alu.max), [("VE", 1)])
        vhop()

    def collective(k):
        op("gpsimd", lambda e, k=k: e.collective_compute(
            "AllReduce", Alu.max, replica_groups=[list(range(8))],
            ins=[ccI[k].ap().opt()], outs=[ccO[k].ap().opt()]), [("CC", 1)])
        return S.now("CC")

    def marks():
        return [(s, S.now(s)) for s in ("TE", "VE", "SE", "S2", "GD")]

    def qgate(mk, engines):
        for e in engines:
            for semnm, val in mk:
                w8(e, semnm, val)

    markP = nc.sbuf_base
    wpr = [sb(f"wpr{i}", [128, 1024], F32) for i in range(2)]
    wqi = sb("wqi", [128, 1024], I32)
    wps = sb("wps", [128, 96], F32)
    brow = sb("brow", [1, 512], F32)
    markEA = nc.sbuf_base

    wstore = {}

    def w_prep_chunk(nm, ci, dequant, sec=0):
        buf = (ci + sec) % 2
        key = buf
        if key in wstore:
            w8("sync", "GD", wstore[key])
        src = w_h[nm].rearrange("(c p) i -> p c i", p=128)[
            :, ci:ci + 1, sec * 1024:(sec + 1) * 1024]
        ld = dma("sync", wpr[buf][:, :].rearrange("p (c i) -> p c i", c=1),
                 src, "LD")
        dst = sWB[nm].rearrange("(c p) i -> p c i", p=128)[
            :, ci:ci + 1, sec * 1024:(sec + 1) * 1024]
        if dequant:
            w8("vector", "LD", ld)
            w3 = wpr[buf][:, :].rearrange("p (b e) -> p b e", e=32)
            q3 = wqi[:, :].rearrange("p (b e) -> p b e", e=32)
            op("vector", lambda e, w3=w3: e.tensor_reduce(
                wps[:, 0:32], w3, AX, Alu.max, apply_absolute_value=True))
            vhop()
            op("vector", lambda e: e.tensor_scalar(
                wps[:, 32:64], wps[:, 0:32], 1.0 / 7.0, 1e-12,
                Alu.mult, Alu.max))
            vhop()
            op("vector", lambda e: e.reciprocal(wps[:, 64:96],
                                                wps[:, 32:64]))
            vhop()
            inv_bc = rep_ap(wps[:, 64:65], 32, 32)
            s_bc = rep_ap(wps[:, 32:33], 32, 32)
            op("vector", lambda e, w3=w3, q3=q3, bc=inv_bc: e.tensor_tensor(
                q3, w3, bc, Alu.mult))
            op("vector", lambda e, w3=w3, q3=q3, bc=s_bc: e.tensor_tensor(
                w3, q3, bc, Alu.mult), [("VE", 1)])
            w8("gpsimd", "VE")
        else:
            w8("gpsimd", "LD", ld)
        wstore[key] = dma(
            "gpsimd", dst,
            wpr[buf][:, :].rearrange("p (c i) -> p c i", c=1), "GD")

    def wT_load(nm, dstT, gd_ready):
        rows, cols = sWB[nm].shape
        w8("scalar", "GD", gd_ready)
        for c in range(cols // 128):
            dmaT("scalar", dstT[:, c * rows:(c + 1) * rows],
                 sWB[nm][:, c * 128:(c + 1) * 128], "S2")
        return S.now("S2")

    def mset(e):
        e.memset(cst[:, 0:1], 1.0 / 1024.0)
        e.memset(cst[:, 1:2], -2.0)
        e.memset(cst[:, 2:3], -0.5)
        e.memset(cst[:, 3:4], 1e-5)
        e.memset(ones1[:, :], 1.0)
        e.memset(id1[:, :], 1.0)
        return e.memset(onecol[:, :], 1.0)
    op("vector", mset, [("VE", 1)])

    op("gpsimd", lambda e: e.iota(
        statMV4[:, 0:NT128], [[128, NT128]], channel_multiplier=1,
        allow_small_or_imprecise_dtypes=True), [("GE", 1)])
    w8("scalar", "GE")
    w8("scalar", "VE")
    op("scalar", lambda e: e.activation(statMV4[:, 0:NT128], statMV4[:, 0:NT128], Act.Square,
                                        bias=cst[:, 1:2], scale=cst[:, 0:1]),
       [("SE", 1)])
    w8("vector", "SE")
    op("vector", lambda e: e.tensor_scalar(statMV4[:, 0:NT128], statMV4[:, 0:NT128],
                                           -0.5, None, Alu.mult),
       [("VE", 1)])
    w8("scalar", "VE")
    op("scalar", lambda e: e.activation(mcol[:, :], statMV4[:, 0:NT128], Act.Exp,
                                        bias=0.0, scale=1.0), [("SE", 1)])
    w8("vector", "SE")
    op("vector", lambda e: e.tensor_scalar(mcol[:, :], mcol[:, :],
                                           1.0 / 32.0, None, Alu.mult),
       [("VE", 1)])

    for bh, dst in ((b_h["bq"], bqF), (b_h["bk"], bkF)):
        bias_ld = dma("sync", dst[:, :],
                      bh.rearrange("one (c p) -> p (one c)", p=128),
                      "LD", slow=True)
    w8("scalar", "LD", bias_ld)

    LNxT = sb("LNxT", [128, DC * T], BF16)
    WqT = sb("WqT", [128, DC * D], BF16)
    WkT = sb("WkT", [128, DC * D], BF16)
    WvT = sb("WvT", [128, DC * D], BF16)
    bvB = sb("bvB", [128, D], F32)
    xr = [sb(f"xr{i}", [128, D], F32) for i in range(2)]
    lnstg = [sb(f"lnstg{i}", [128, D], BF16) for i in range(2)]
    qstg = sb("qstg", [128, DC * 512], BF16)
    kstg = sb("kstg", [128, DC * 512], BF16)
    vstg = [sb(f"vstg{i}", [128, D], BF16) for i in range(2)]

    def LNxTv(c):
        return LNxT[:, c * T:(c + 1) * T]

    def WTv(t, c):
        return t[:, c * D:(c + 1) * D]

    w8("tensor", "VE", 1)
    for h in range(2):
        if h:
            w8("sync", "TE")
        ld_bv = dma("sync", brow[:, :], b_h["bv"][:, h * 512:(h + 1) * 512],
                    "LD")
        w8("tensor", "LD", ld_bv)
        bank_wait("pM")
        mm(pM[:, 0:512], ones1[:, :], brow[:, :],
           True, True, inc=True)
        w8("vector", "TE")
        op("vector", lambda e, h=h: e.tensor_copy(
            bvB[:, h * 512:(h + 1) * 512], pM[:, 0:512]), [("VE", 1)])
        bank_done("pM", "VE")

    ln_done = [None] * NT128
    xr_free = [None, None]
    lnxT_ready = [0] * 4
    qk_stg_free = {"q": None, "k": None}
    vstg_free = [None, None]
    ln_sqrt_se = [None]

    def ln1_tile(tt):
        buf = tt % 2
        if xr_free[buf] is not None:
            w8("sync", "VE", xr_free[buf])
        ld = dma("sync", xr[buf][:, :], x_h[tt * 128:(tt + 1) * 128, :],
                 "LD")
        w8("vector", "LD", ld)
        if tt >= 2:
            w8("vector", "S2", ln_done[tt - 2])
        if ln_sqrt_se[0] is not None:
            w8("vector", "SE", ln_sqrt_se[0])
        op("vector", lambda e, b=buf: e.bn_stats(lnst[:, 0:6],
                                                 xr[b][:, 0:512]))
        op("vector", lambda e, b=buf: e.bn_stats(lnst[:, 6:12],
                                                 xr[b][:, 512:1024]))
        op("vector", lambda e: e.bn_aggr(lnag[:, 0:2], lnst[:, 0:6]))
        op("vector", lambda e: e.bn_aggr(lnag[:, 2:4], lnst[:, 6:12]))
        vhop()
        op("vector", lambda e: e.tensor_tensor(lnsc[:, 0:1], lnag[:, 0:1],
                                               lnag[:, 2:3], Alu.add))
        op("vector", lambda e: e.tensor_tensor(lnsc[:, 1:2], lnag[:, 0:1],
                                               lnag[:, 2:3], Alu.subtract))
        op("vector", lambda e: e.tensor_tensor(lnsc[:, 2:3], lnag[:, 1:2],
                                               lnag[:, 3:4], Alu.add))
        vhop()
        op("vector", lambda e: e.tensor_scalar(lnsc[:, 3:4], lnsc[:, 0:1],
                                               0.5, None, Alu.mult))
        op("vector", lambda e: e.tensor_tensor(lnsc[:, 4:5], lnsc[:, 1:2],
                                               lnsc[:, 1:2], Alu.mult))
        op("vector", lambda e: e.tensor_scalar(lnsc[:, 5:6], lnsc[:, 2:3],
                                               0.5, None, Alu.mult))
        vhop()
        op("vector", lambda e: e.scalar_tensor_tensor(
            lnsc[:, 6:7], lnsc[:, 4:5], 0.25, lnsc[:, 5:6],
            Alu.mult, Alu.add), [("VE", 1)])
        w8("scalar", "VE")
        op("scalar", lambda e: e.activation(lntmp[:, 0:1], lnsc[:, 6:7],
                                            Act.Sqrt, bias=cst[:, 3:4]),
           [("SE", 1)])
        ln_sqrt_se[0] = S.now("SE")
        w8("vector", "SE")
        op("vector", lambda e: e.reciprocal(lntmp[:, 1:2], lntmp[:, 0:1]))
        vhop()
        op("vector", lambda e, b=buf: e.tensor_scalar(
            lnstg[b][:, :], xr[b][:, :], lnsc[:, 3:4], lntmp[:, 1:2],
            Alu.subtract, Alu.mult), [("VE", 1)])
        xr_free[buf] = S.now("VE")
        w8("scalar", "VE")
        ln_done[tt] = dma("scalar", sLNX[tt * 128:(tt + 1) * 128, :],
                          lnstg[buf][:, :], "S2")

    def lnxT_quarter(qr):
        w8("scalar", "S2", ln_done[qr * 8 + 7])
        for c in range(DC):
            dmaT("scalar", LNxTv(c)[:, qr * 1024:(qr + 1) * 1024],
                 sLNX[qr * 1024:(qr + 1) * 1024, c * 128:(c + 1) * 128],
                 "S2")
        lnxT_ready[qr] = S.now("S2")

    wT_s2s = {}

    def qkv_t512(i, wT_gate):
        w8("tensor", "S2", max(lnxT_ready[i // 2], wT_s2s["wk"]))
        for which in ("k", "q"):
            if which == "q":
                w8("tensor", "S2", wT_s2s["wq"])
            wT = WkT if which == "k" else WqT
            bF = bkF if which == "k" else bqF
            stg = kstg if which == "k" else qstg
            sdst = sK if which == "k" else sQ
            if qk_stg_free[which] is not None:
                w8("scalar", "S2", qk_stg_free[which])
            for oc in range(DC):
                bank, pbank = (("pA", pA) if oc % 2 == 0 else ("pB", pB))
                bank_wait(bank)
                for icc in range(DC):
                    mm(pbank[:, 0:512],
                       WTv(wT, icc)[:, oc * 128:(oc + 1) * 128],
                       LNxTv(icc)[:, i * 512:(i + 1) * 512],
                       icc == 0, icc == DC - 1, inc=(icc == DC - 1))
                w8("scalar", "TE")
                op("scalar", lambda e, oc=oc, pb=pbank, st=stg, bf=bF:
                   e.activation(st[:, oc * 512:(oc + 1) * 512], pb[:, 0:512],
                                Act.Identity, bias=bf[:, oc:oc + 1]),
                   [("SE", 1)])
                bank_done(bank, "SE")
            w8("scalar", "SE")
            qk_stg_free[which] = dma(
                "scalar",
                sdst.rearrange("(c p) t -> p c t", p=128)
                [:, :, i * 512:(i + 1) * 512],
                stg[:, :].rearrange("p (c t) -> p c t", c=DC), "S2")
        w8("tensor", "S2", wT_s2s["wv"])
        for tj in range(4):
            tt = i * 4 + tj
            vb = tj % 2
            bank, pbank = (("pF", pF) if vb == 0 else ("pG", pG))
            if vstg_free[vb] is not None:
                w8("vector", "S2", vstg_free[vb])
            for h in range(2):
                bank_wait(bank)
                for icc in range(DC):
                    mm(pbank[:, 0:512],
                       LNxTv(icc)[:, tt * 128:(tt + 1) * 128],
                       WTv(WvT, icc)[:, h * 512:(h + 1) * 512],
                       icc == 0, icc == DC - 1, inc=(icc == DC - 1))
                w8("vector", "TE")
                op("vector", lambda e, vb=vb, h=h, pb=pbank: e.tensor_tensor(
                    vstg[vb][:, h * 512:(h + 1) * 512], pb[:, 0:512],
                    bvB[:, h * 512:(h + 1) * 512], Alu.add), [("VE", 1)])
                bank_done(bank, "VE")
            w8("scalar", "VE")
            vstg_free[vb] = dma("scalar", sV[tt * 128:(tt + 1) * 128, :],
                                vstg[vb][:, :], "S2")

    prep_attn = [(nm, ci) for nm in ("wk", "wq", "wv") for ci in range(DC)]
    wT_attn = 0
    for qr in range(4):
        for tt in range(qr * 8, qr * 8 + 8):
            ln1_tile(tt)
            if qr == 0:
                for nm, ci in prep_attn[3 * (tt % 8): 3 * (tt % 8) + 3]:
                    w_prep_chunk(nm, ci, False)
        lnxT_quarter(qr)
        if qr == 0:
            gd_all = S.now("GD")
            wT_s2s["wk"] = wT_load("wk", WkT, gd_all)
            wT_s2s["wq"] = wT_load("wq", WqT, gd_all)
            wT_attn = wT_s2s["wv"] = wT_load("wv", WvT, gd_all)
        if qr >= 1:
            qkv_t512(2 * (qr - 1), wT_attn)
            qkv_t512(2 * qr - 1, wT_attn)
    qkv_t512(6, wT_attn)
    qkv_t512(7, wT_attn)

    mA1 = marks()

    nc.sbuf_base = markEA
    KbT = sb("KbT", [128, DC * T], BF16)
    Vres = sb("Vres", [128, NT128 * D], BF16)
    PT = sb("PT", [128, NT128 * QB], BF16)
    qtl = sb("qtl", [128, DC * QB], BF16)
    atoT = sb("atoT", [128, DC * QB], BF16)
    denrow = sb("denrow", [1, QB], F32)
    denrec = sb("denrec", [1, QB], F32)
    rdenT = sb("rdenT", [128, QB], F32)
    atl = [sb(f"atl{i}", [128, DC * 128], BF16) for i in range(3)]

    def KbTv(c):
        return KbT[:, c * T:(c + 1) * T]

    def Vresv(k):
        return Vres[:, k * D:(k + 1) * D]

    def PTv(k):
        return PT[:, k * QB:(k + 1) * QB]

    qgate(mA1, ("sync", "scalar", "gpsimd"))

    for ci in range(DC):
        w_prep_chunk("wo", ci, False)
    wo_gd = S.now("GD")

    kb_ld = dma("sync", KbT[:, :].rearrange("p (c t) -> p c t", c=DC),
                sK.rearrange("(c p) t -> p c t", p=128), "LD")
    vres_ld = dma("sync", Vres[:, :].rearrange("p (k d) -> p k d", k=NT128),
                  sV.rearrange("(k p) d -> p k d", p=128), "LD")

    qt_ld = [None]
    qt_ld[0] = dma("sync", qtl[:, :].rearrange("p (c t) -> p c t", c=DC),
                   sQ.rearrange("(c p) t -> p c t", p=128)[:, :, 0:QB], "LD")

    exp_cnt = {}
    ato_st = [None] * NQ

    w8("tensor", "VE")
    w8("tensor", "SE")

    def a2_qblock(q):
        w8("tensor", "LD", qt_ld[0])
        if q == 0:
            w8("tensor", "LD", kb_ld)
        for k in range(NT128):
            bank, pbank = (("pA", pA) if k % 2 == 0 else ("pB", pB))
            bank_wait(bank)
            for icc in range(DC):
                mm(pbank[:, 0:QB],
                   KbTv(icc)[:, k * 128:(k + 1) * 128],
                   qtl[:, icc * QB:(icc + 1) * QB],
                   icc == 0, icc == DC - 1, inc=(icc == DC - 1))
            w8("scalar", "TE")
            op("scalar", lambda e, k=k, pb=pbank: e.activation(
                PTv(k)[:, :], pb[:, 0:QB], Act.Exp,
                bias=0.0, scale=mcol[:, k:k + 1]), [("SE", 1)])
            exp_cnt[k] = S.now("SE")
            bank_done(bank, "SE")
        s_te = S.now("TE")
        if q + 1 < NQ:
            w8("sync", "TE", s_te)
            qt_ld[0] = dma(
                "sync", qtl[:, :].rearrange("p (c t) -> p c t", c=DC),
                sQ.rearrange("(c p) t -> p c t", p=128)
                [:, :, (q + 1) * QB:(q + 2) * QB], "LD")
        if q == 0:
            w8("tensor", "LD", vres_ld)
        bank_wait("pD")
        for k in range(NT128):
            w8("tensor", "SE", exp_cnt[k])
            mm(pD[0:1, 0:QB], onecol[:, :], PTv(k)[:, :],
               k == 0, k == NT128 - 1, inc=(k == NT128 - 1))
        w8("vector", "TE")
        op("vector", lambda e: e.tensor_copy(denrow[:, :], pD[0:1, 0:QB]),
           [("VE", 1)])
        bank_done("pD", "VE")
        vhop()
        op("vector", lambda e: e.reciprocal(denrec[:, :], denrow[:, :]),
           [("VE", 1)])
        den_ve = S.now("VE")
        bank_wait("pM")
        w8("tensor", "VE", den_ve)
        mm(pM[:, 0:QB], ones1[:, :], denrec[:, :], True, True, inc=True)
        w8("vector", "TE")
        op("vector", lambda e: e.tensor_copy(rdenT[:, :], pM[:, 0:QB]),
           [("VE", 1)])
        vhop()
        bank_done("pM", "VE")
        pvb = (("pC", pC), ("pE", pE), ("pF", pF), ("pG", pG))
        for dcc in range(DC):
            bank, pbank = pvb[dcc % 4]
            bank_wait(bank)
            for k in range(NT128):
                mm(pbank[:, 0:QB],
                   Vresv(k)[:, dcc * 128:(dcc + 1) * 128],
                   PTv(k)[:, :],
                   k == 0, k == NT128 - 1, inc=(k == NT128 - 1))
            w8("vector", "TE")
            if dcc == 0 and q > 0:
                w8("vector", "S2", ato_st[q - 1])
            op("vector", lambda e, dcc=dcc, pb=pbank: e.tensor_tensor(
                atoT[:, dcc * QB:(dcc + 1) * QB], pb[:, 0:QB],
                rdenT[:, :], Alu.mult), [("VE", 1)])
            bank_done(bank, "VE")
        w8("scalar", "VE")
        ato_st[q] = dma(
            "scalar",
            sATO.rearrange("(c p) t -> p c t", p=128)
            [:, :, q * QB:(q + 1) * QB],
            atoT[:, :].rearrange("p (c t) -> p c t", c=DC), "S2")

    dq_chunks = ([("wup", ci, 0) for ci in range(ICN)]
                 + [("wgate", ci, 0) for ci in range(ICN)]
                 + [("wdown", ci, sec) for ci in range(DC)
                    for sec in range(3)])
    per_q = (len(dq_chunks) + NQ - 1) // NQ
    for q in range(NQ):
        a2_qblock(q)
        for nm, ci, sec in dq_chunks[q * per_q:(q + 1) * per_q]:
            w_prep_chunk(nm, ci, True, sec)
    dq_gd = S.now("GD")

    atl_ld = [None] * NT128
    for tt in range(3):
        w8("sync", "S2", ato_st[tt // 4])
        atl_ld[tt] = dma(
            "sync", atl[tt % 3][:, :].rearrange("p (c t) -> p c t", c=DC),
            sATO.rearrange("(c p) t -> p c t", p=128)
            [:, :, tt * 128:(tt + 1) * 128], "LD")

    mA2 = marks()

    nc.sbuf_base = markEA
    WoT = sb("WoT", [128, DC * D], BF16)
    boB = sb("boB", [128, D], F32)
    xr2 = [sb(f"xr2_{i}", [128, D], F32) for i in range(2)]
    xrb = [sb(f"xrb{i}", [128, D], F32) for i in range(2)]
    x1st = [sb(f"x1st{i}", [128, D], F32) for i in range(2)]

    qgate(mA2, ("sync", "scalar", "gpsimd"))
    woT_s2 = wT_load("wo", WoT, wo_gd)

    for h in range(2):
        if h:
            w8("sync", "TE")
        ld_bo = dma("sync", brow[:, :], b_h["bo"][:, h * 512:(h + 1) * 512],
                    "LD")
        w8("tensor", "LD", ld_bo)
        bank_wait("pM")
        mm(pM[:, 0:512], ones1[:, :], brow[:, :],
           True, True, inc=True)
        w8("vector", "TE")
        op("vector", lambda e, h=h: e.tensor_copy(
            boB[:, h * 512:(h + 1) * 512], pM[:, 0:512]), [("VE", 1)])
        bank_done("pM", "VE")
    boB_ve = S.now("VE")

    xr2_free = [None, None]
    x1_store = [None, None]
    xrb_ve = [None, None]
    xrb_ge = [None, None]
    atl_te = [None] * NT128

    for tt in range(NT128):
        buf = tt % 2
        if xr2_free[buf] is not None:
            w8("gpsimd", "GE", xr2_free[buf])
        xld = dma("gpsimd", xr2[buf][:, :], x_h[tt * 128:(tt + 1) * 128, :],
                  "GD")
        w8("gpsimd", "GD", xld)
        w8("gpsimd", "VE", xrb_ve[buf] if xrb_ve[buf] is not None
           else boB_ve)
        op("gpsimd", lambda e, b=buf: e.tensor_tensor(
            xrb[b][:, :], xr2[b][:, :], boB[:, :], Alu.add), [("GE", 1)])
        xr2_free[buf] = xrb_ge[buf] = S.now("GE")
        if DEBUG_SERIAL:
            for sem in ("LD", "S2", "GE", "VE", "SE", "TE"):
                for eng in ("tensor", "vector", "scalar", "gpsimd", "sync"):
                    w8(eng, sem)
        w8("tensor", "LD", atl_ld[tt])
        if tt == 0:
            w8("tensor", "S2", woT_s2)
        for h in range(2):
            bank, pbank = (("pF", pF) if h == 0 else ("pG", pG))
            bank_wait(bank)
            for icc in range(DC):
                mm(pbank[:, 0:512],
                   atl[tt % 3][:, icc * 128:(icc + 1) * 128],
                   WTv(WoT, icc)[:, h * 512:(h + 1) * 512],
                   icc == 0, icc == DC - 1, inc=(icc == DC - 1))
            w8("vector", "TE")
            if h == 0:
                w8("vector", "GE", xrb_ge[buf])
                if x1_store[buf] is not None:
                    w8("vector", "S2", x1_store[buf])
            op("vector", lambda e, h=h, pb=pbank, b=buf: e.tensor_tensor(
                x1st[b][:, h * 512:(h + 1) * 512], pb[:, 0:512],
                xrb[b][:, h * 512:(h + 1) * 512], Alu.add), [("VE", 1)])
            bank_done(bank, "VE")
        atl_te[tt] = S.now("TE")
        xrb_ve[buf] = S.now("VE")
        if tt + 3 < NT128:
            w8("sync", "TE", atl_te[tt])
            w8("sync", "S2", ato_st[(tt + 3) // 4])
            atl_ld[tt + 3] = dma(
                "sync",
                atl[(tt + 3) % 3][:, :].rearrange("p (c t) -> p c t", c=DC),
                sATO.rearrange("(c p) t -> p c t", p=128)
                [:, :, (tt + 3) * 128:(tt + 4) * 128], "LD")
        op("vector", lambda e, b=buf: e.bn_stats(lnst[:, 0:6],
                                                 x1st[b][:, 0:512]))
        op("vector", lambda e, b=buf: e.bn_stats(lnst[:, 6:12],
                                                 x1st[b][:, 512:1024]))
        op("vector", lambda e, tt=tt: e.bn_aggr(
            statMV4[:, 4 * tt:4 * tt + 2], lnst[:, 0:6]))
        op("vector", lambda e, tt=tt: e.bn_aggr(
            statMV4[:, 4 * tt + 2:4 * tt + 4], lnst[:, 6:12]))
        op("vector", lambda e, tt=tt, b=buf: e.tensor_reduce(
            maxC[:, tt:tt + 1], x1st[b][:, :], AX, Alu.max))
        op("vector", lambda e, tt=tt, b=buf: e.tensor_reduce(
            minC[:, tt:tt + 1], x1st[b][:, :], AX, Alu.min), [("VE", 1)])
        ve_now = S.now("VE")
        w8("scalar", "VE", ve_now)
        x1_store[buf] = dma("scalar", sX1[tt * 128:(tt + 1) * 128, :],
                            x1st[buf][:, :], "S2")

    mA2 = marks()

    w8("vector", "GE")
    s4 = statMV4[:, :].rearrange("p (t f) -> p t f", f=4)
    mAv, vAv, mBv, vBv = (s4[:, :, i:i + 1] for i in range(4))
    b3 = bscr[:, :].rearrange("p (k t) -> p k t", k=3)
    t0v, t1v, t2v = (b3[:, i, :] for i in range(3))
    op("vector", lambda e: e.tensor_tensor(
        t0v.rearrange("p (t o) -> p t o", o=1), mAv, mBv, Alu.add))
    op("vector", lambda e: e.tensor_tensor(
        t1v.rearrange("p (t o) -> p t o", o=1), mAv, mBv, Alu.subtract))
    op("vector", lambda e: e.tensor_tensor(
        t2v.rearrange("p (t o) -> p t o", o=1), vAv, vBv, Alu.add))
    vhop()
    op("vector", lambda e: e.tensor_scalar(statM2[:, :], t0v, 0.5, None,
                                           Alu.mult))
    op("vector", lambda e: e.tensor_tensor(t0v, t1v, t1v, Alu.mult))
    op("vector", lambda e: e.tensor_scalar(statV2[:, :], t2v, 0.5, None,
                                           Alu.mult))
    vhop()
    op("vector", lambda e: e.scalar_tensor_tensor(
        statV2[:, :], t0v, 0.25, statV2[:, :], Alu.mult, Alu.add),
       [("VE", 1)])
    w8("scalar", "VE")
    op("scalar", lambda e: e.activation(statR[:, :], statV2[:, :],
                                        Act.Sqrt, bias=cst[:, 3:4]),
       [("SE", 1)])
    w8("vector", "SE")
    op("vector", lambda e: e.reciprocal(statR[:, :], statR[:, :]))
    vhop()
    op("vector", lambda e: e.tensor_tensor(t0v, maxC[:, :], statM2[:, :],
                                           Alu.subtract))
    op("vector", lambda e: e.tensor_tensor(t1v, statM2[:, :], minC[:, :],
                                           Alu.subtract))
    vhop()
    op("vector", lambda e: e.tensor_tensor(t2v, t0v, t1v, Alu.max))
    vhop()
    op("vector", lambda e: e.tensor_tensor(t2v, t2v, statR[:, :], Alu.mult))
    vhop()
    op("vector", lambda e: e.tensor_reduce(smax[:, 0:1], t2v, AX,
                                           Alu.max), [("VE", 1)])
    w8("gpsimd", "VE")
    dma("gpsimd", ccI[1].ap(), smax[:, 0:1], "GD")
    w8("gpsimd", "GD")
    collective(1)
    w8("gpsimd", "CC")
    g = dma("gpsimd", armaxs[:, 0:1], ccO[1].ap(), "GD")
    w8("gpsimd", "GD", g)
    cross_part_max(armaxs[:, 0:1], sc1[:, 0:1])
    op("vector", lambda e: e.tensor_scalar(sc1[:, 1:2], sc1[:, 0:1],
                                           1.0 / 32767.0, 1e-12,
                                           Alu.mult, Alu.max))
    vhop()
    op("vector", lambda e: e.reciprocal(sc1[:, 2:3], sc1[:, 1:2]),
       [("VE", 1)])
    vhop()
    bcast(sc1[:, 1:3], scB[:, 0:2])
    vhop()
    op("vector", lambda e: e.tensor_scalar(aB[:, :], statR[:, :],
                                           scB[:, 1:2], None, Alu.mult))
    vhop()
    op("vector", lambda e: e.tensor_tensor(bBv[:, :], statM2[:, :],
                                           aB[:, :], Alu.mult),
       [("VE", 1)])
    vhop()

    nc.sbuf_base = markP
    xqT = sb("xqT", [128, DC * T], BF16)
    WuT = sb("WuT", [128, DC * DI], BF16)
    WgT = sb("WgT", [128, DC * DI], BF16)
    slab = [sb(f"slab{i}", [128, 2048], F32) for i in range(2)]
    x1r = [sb(f"x1r{i}", [128, D], F32) for i in range(2)]
    xqi = sb("xqi", [128, D], I32)
    xqs = [sb(f"xqs{i}", [128, D], BF16) for i in range(2)]

    def xqTv(c):
        return xqT[:, c * T:(c + 1) * T]

    def WiTv(t, c):
        return t[:, c * DI:(c + 1) * DI]

    qgate(mA2, ("sync", "scalar", "gpsimd"))
    wuT_s2 = wT_load("wup", WuT, dq_gd)

    xq_done = [None] * NT128
    x1r_free = [None, None]
    xq_ve = [None] * NT128
    xqT_ready = [0] * 4

    for qr in range(4):
        for tt in range(qr * 8, qr * 8 + 8):
            buf = tt % 2
            if x1r_free[buf] is not None:
                w8("sync", "VE", x1r_free[buf])
            ld = dma("sync", x1r[buf][:, :],
                     sX1[tt * 128:(tt + 1) * 128, :], "LD")
            w8("vector", "LD", ld)
            if tt >= 2:
                w8("vector", "S2", xq_done[tt - 2])
            op("vector", lambda e, b=buf, tt=tt: e.tensor_scalar(
                xqi[:, :], x1r[b][:, :], aB[:, tt:tt + 1],
                bBv[:, tt:tt + 1], Alu.mult, Alu.subtract))
            op("vector", lambda e, b=buf: e.tensor_scalar(
                xqs[b][:, :], xqi[:, :], scB[:, 0:1], None, Alu.mult),
               [("VE", 1)])
            x1r_free[buf] = S.now("VE")
            w8("scalar", "VE")
            xq_done[tt] = dma("scalar", sXQ[tt * 128:(tt + 1) * 128, :],
                              xqs[buf][:, :], "S2")
        w8("scalar", "S2", xq_done[qr * 8 + 7])
        for c in range(DC):
            dmaT("scalar", xqTv(c)[:, qr * 1024:(qr + 1) * 1024],
                 sXQ[qr * 1024:(qr + 1) * 1024, c * 128:(c + 1) * 128],
                 "S2")
        xqT_ready[qr] = S.now("S2")

    wgT_s2 = wT_load("wgate", WgT, dq_gd)

    slab_free = [None, None]

    def f1_pass(wT, sdst, rcol, wT_s2):
        w8("tensor", "S2", wT_s2)
        for icc in range(ICN):
            for hh in range(2):
                buf = hh
                if slab_free[buf] is not None:
                    for semnm, cv in slab_free[buf]:
                        w8("scalar", semnm, cv)
                for ii in range(4):
                    i = hh * 4 + ii
                    if icc == 0:
                        w8("tensor", "S2", xqT_ready[i // 2])
                    bank, pbank = (("pA", pA) if i % 2 == 0 else ("pB", pB))
                    bank_wait(bank)
                    for h in range(DC):
                        mm(pbank[:, 0:512],
                           WiTv(wT, h)[:, icc * 128:(icc + 1) * 128],
                           xqTv(h)[:, i * 512:(i + 1) * 512],
                           h == 0, h == DC - 1, inc=(h == DC - 1))
                    w8("scalar", "TE")
                    op("scalar", lambda e, b=buf, ii=ii, pb=pbank: e.copy(
                        slab[b][:, ii * 512:(ii + 1) * 512], pb[:, 0:512]),
                       [("SE", 1)])
                    bank_done(bank, "SE")
                w8("vector", "SE")
                op("vector", lambda e, b=buf, icc=icc, hh=hh, rc=rcol:
                   e.tensor_reduce(rc[:, 2 * icc + hh:2 * icc + hh + 1],
                                   slab[b][:, :], AX, Alu.max,
                                   apply_absolute_value=True), [("VE", 1)])
                w8("scalar", "SE")
                st = dma("scalar",
                         sdst.rearrange("(c p) t -> p c t", p=128)
                         [:, icc:icc + 1, hh * 2048:(hh + 1) * 2048],
                         slab[buf][:, :].rearrange("p (c t) -> p c t", c=1),
                         "S2")
                slab_free[buf] = [("VE", S.now("VE")), ("S2", st)]

    f1_pass(WuT, sUP, rupC, wuT_s2)
    f1_pass(WgT, sGA, rgaC, wgT_s2)
    mF1 = marks()

    op("vector", lambda e: e.tensor_reduce(smax[:, 0:1], rupC[:, :], AX,
                                           Alu.max))
    op("vector", lambda e: e.tensor_reduce(smax[:, 1:2], rgaC[:, :], AX,
                                           Alu.max), [("VE", 1)])
    w8("gpsimd", "VE")
    dma("gpsimd", ccI[2].ap(), smax[:, 0:2], "GD")
    w8("gpsimd", "GD")
    collective(2)
    w8("gpsimd", "CC")
    g = dma("gpsimd", armaxs[:, 0:2], ccO[2].ap(), "GD")
    w8("gpsimd", "GD", g)
    cross_part_max(armaxs[:, 0:1], sc1[:, 3:4])
    cross_part_max(armaxs[:, 1:2], sc1[:, 6:7])
    op("vector", lambda e: e.tensor_scalar(sc1[:, 4:5], sc1[:, 3:4],
                                           1.0 / 32767.0, 1e-12,
                                           Alu.mult, Alu.max))
    op("vector", lambda e: e.tensor_scalar(sc1[:, 7:8], sc1[:, 6:7],
                                           1.0 / 32767.0, 1e-12,
                                           Alu.mult, Alu.max))
    vhop()
    op("vector", lambda e: e.reciprocal(sc1[:, 5:6], sc1[:, 4:5]))
    vhop()
    op("vector", lambda e: e.reciprocal(sc1[:, 8:9], sc1[:, 7:8]),
       [("VE", 1)])
    vhop()
    bcast(sc1[:, 5:6], scB[:, 2:3])
    bcast(sc1[:, 7:9], scB[:, 4:6])
    scB_ve_b2 = S.now("VE")

    nc.sbuf_base = markP
    NH = 2048
    gar = [sb(f"gar{i}", [128, NH], F32) for i in range(2)]
    upr = [sb(f"upr{i}", [128, NH], F32) for i in range(4)]
    tg = [sb(f"tg{i}", [128, NH], I32) for i in range(3)]
    sg_ = [sb(f"sg_{i}", [128, NH], F32) for i in range(2)]
    ts_ = [sb(f"ts_{i}", [128, NH], I32) for i in range(2)]
    th = [sb(f"th{i}", [128, NH], I32) for i in range(3)]
    tg2 = [sb(f"tg2{i}", [128, NH], I32) for i in range(2)]
    tup = [sb(f"tup{i}", [128, NH], I32) for i in range(2)]
    ott = [sb(f"ott{i}", [128, NH], F32) for i in range(2)]

    qgate(mF1, ("sync", "gpsimd"))

    UN = 2 * ICN
    guards = {}

    def guard_w(eng, key):
        for semnm, cv in guards.get(key, ()):
            w8(eng, semnm, cv)

    def gslab(icc, hh):
        return (sGA.rearrange("(c p) t -> p c t", p=128)
                [:, icc:icc + 1, hh * NH:(hh + 1) * NH])

    def uslab(icc, hh):
        return (sUP.rearrange("(c p) t -> p c t", p=128)
                [:, icc:icc + 1, hh * NH:(hh + 1) * NH])

    def oslab(icc, hh):
        return (sOT.rearrange("(c p) t -> p c t", p=128)
                [:, icc:icc + 1, hh * NH:(hh + 1) * NH])

    r1 = lambda t: t[:, :].rearrange("p (c t) -> p c t", c=1)

    def stage_gload(u):
        b2 = u % 2
        guard_w("sync", ("gar", b2))
        lds["g", u] = dma("sync", r1(gar[b2]), gslab(u // 2, u % 2), "LD")

    def stage_uload(u):
        b4 = u % 4
        guard_w("gpsimd", ("upr", b4))
        lds["u", u] = dma("gpsimd", r1(upr[b4]), uslab(u // 2, u % 2), "GD")

    def stage_s1(u):
        w8("scalar", "LD", lds["g", u])
        guard_w("scalar", ("tg", u % 3))
        op("scalar", lambda e, b=u % 2, o=u % 3: e.activation(
            tg[o][:, :], gar[b][:, :], Act.Identity, bias=0.0,
            scale=scB[:, 5:6]), [("SE", 1)])
        guard_w("scalar", ("sg_", u % 2))
        op("scalar", lambda e, o3=u % 3, o=u % 2: e.activation(
            sg_[o][:, :], tg[o3][:, :], Act.Sigmoid, bias=0.0,
            scale=scB[:, 4:5]), [("SE", 1)])
        sems_u["sig", u] = S.now("SE")
        guards[("gar", u % 2)] = [("SE", sems_u["sig", u])]

    def stage_v1(u):
        w8("vector", "SE", sems_u["sig", u])
        guard_w("vector", ("ts_", u % 2))
        op("vector", lambda e, i=u % 2, o=u % 2: e.tensor_scalar(
            ts_[o][:, :], sg_[i][:, :], 65536.0, 65535.0,
            Alu.mult, Alu.min), [("VE", 1)])
        sems_u["ts", u] = S.now("VE")
        guards[("sg_", u % 2)] = [("VE", sems_u["ts", u])]

    def stage_g1(u):
        w8("gpsimd", "VE", sems_u["ts", u])
        guard_w("gpsimd", ("th", u % 3))
        op("gpsimd", lambda e, i=u % 2, i3=u % 3, o=u % 3: e.tensor_tensor(
            th[o][:, :], ts_[i][:, :], tg[i3][:, :], Alu.mult), [("GE", 1)])
        sems_u["th", u] = S.now("GE")
        guards[("ts_", u % 2)] = [("GE", sems_u["th", u])]
        guards[("tg", u % 3)] = [("GE", sems_u["th", u])]

    def stage_f2red(u):
        w8("vector", "GE", sems_u["th", u])
        op("vector", lambda e, o=u % 3, u=u: e.tensor_reduce(
            rthC[:, u:u + 1], th[o][:, :], AX, Alu.max,
            apply_absolute_value=True), [("VE", 1)])
        guards[("th", u % 3)] = [("VE", S.now("VE"))]

    def stage_s2(u):
        w8("scalar", "GE", sems_u["th", u])
        w8("scalar", "GD", lds["u", u])
        guard_w("scalar", ("tg2", u % 2))
        op("scalar", lambda e, i=u % 3, o=u % 2: e.activation(
            tg2[o][:, :], th[i][:, :], Act.Identity, bias=0.0,
            scale=scB[:, 6:7]), [("SE", 1)])
        guards[("th", u % 3)] = [("SE", S.now("SE"))]
        guard_w("scalar", ("tup", u % 2))
        op("scalar", lambda e, b=u % 4, o=u % 2: e.activation(
            tup[o][:, :], upr[b][:, :], Act.Identity, bias=0.0,
            scale=scB[:, 2:3]), [("SE", 1)])
        sems_u["tup", u] = S.now("SE")
        guards[("upr", u % 4)] = [("SE", sems_u["tup", u])]

    def stage_v2(u):
        w8("vector", "SE", sems_u["tup", u])
        guard_w("vector", ("ott", u % 2))
        op("vector", lambda e, o=u % 2: e.tensor_tensor(
            ott[o][:, :], tg2[o][:, :], tup[o][:, :], Alu.mult))
        op("vector", lambda e, o=u % 2, u=u: e.tensor_reduce(
            rotC[:, u:u + 1], ott[o][:, :], AX, Alu.max,
            apply_absolute_value=True), [("VE", 1)])
        sems_u["ot", u] = S.now("VE")
        guards[("tg2", u % 2)] = [("VE", sems_u["ot", u])]
        guards[("tup", u % 2)] = [("VE", sems_u["ot", u])]

    def stage_store(u):
        w8("scalar", "VE", sems_u["ot", u])
        st = dma("scalar", oslab(u // 2, u % 2), r1(ott[u % 2]), "S2")
        guards[("ott", u % 2)] = [("VE", sems_u["ot", u]), ("S2", st)]

    lds = {}
    sems_u = {}

    w8("scalar", "VE", scB_ve_b2)
    for i in range(UN + 4):
        if i < UN:
            stage_gload(i)
        if 0 <= i - 1 < UN:
            stage_s1(i - 1)
        if 0 <= i - 2 < UN:
            stage_v1(i - 2)
        if 0 <= i - 3 < UN:
            stage_g1(i - 3)
        if 0 <= i - 4 < UN:
            stage_f2red(i - 4)

    vhop()
    op("vector", lambda e: e.tensor_reduce(smax[:, 0:1], rthC[:, :], AX,
                                           Alu.max), [("VE", 1)])
    w8("gpsimd", "VE")
    dma("gpsimd", ccI[3].ap(), smax[:, 0:1], "GD")
    w8("gpsimd", "GD")
    collective(3)
    w8("gpsimd", "CC")
    g = dma("gpsimd", armaxs[:, 0:1], ccO[3].ap(), "GD")
    w8("gpsimd", "GD", g)
    cross_part_max(armaxs[:, 0:1], sc1[:, 9:10])
    op("vector", lambda e: e.tensor_scalar(sc1[:, 10:11], sc1[:, 7:8],
                                           1.0 / 65536.0, None, Alu.mult))
    vhop()
    op("vector", lambda e: e.tensor_tensor(sc1[:, 11:12], sc1[:, 10:11],
                                           sc1[:, 9:10], Alu.mult))
    vhop()
    op("vector", lambda e: e.tensor_scalar(sc1[:, 12:13], sc1[:, 11:12],
                                           1.0 / 32767.0, 1e-12,
                                           Alu.mult, Alu.max))
    vhop()
    op("vector", lambda e: e.reciprocal(sc1[:, 13:14], sc1[:, 12:13]))
    vhop()
    op("vector", lambda e: e.tensor_tensor(sc1[:, 14:15], sc1[:, 10:11],
                                           sc1[:, 13:14], Alu.mult),
       [("VE", 1)])
    vhop()
    bcast(sc1[:, 14:15], scB[:, 6:7])
    b3_ve = S.now("VE")
    w8("scalar", "VE", b3_ve)

    for i in range(UN + 6):
        if i < UN:
            stage_gload(i)
        if 0 <= i - 1 < UN:
            stage_s1(i - 1)
        if 0 <= i - 2 < UN:
            stage_uload(i - 2)
            stage_v1(i - 2)
        if 0 <= i - 3 < UN:
            stage_g1(i - 3)
        if 0 <= i - 5 < UN:
            stage_s2(i - 5)
        if 0 <= i - 6 < UN:
            stage_v2(i - 6)
            stage_store(i - 6)

    mF3 = marks()

    vhop()
    op("vector", lambda e: e.tensor_reduce(smax[:, 0:1], rotC[:, :], AX,
                                           Alu.max), [("VE", 1)])
    w8("gpsimd", "VE")
    dma("gpsimd", ccI[4].ap(), smax[:, 0:1], "GD")
    w8("gpsimd", "GD")
    collective(4)
    w8("gpsimd", "CC")
    g = dma("gpsimd", armaxs[:, 0:1], ccO[4].ap(), "GD")
    w8("gpsimd", "GD", g)
    cross_part_max(armaxs[:, 0:1], sc1[:, 15:16])
    op("vector", lambda e: e.tensor_tensor(sc1[:, 16:17], sc1[:, 12:13],
                                           sc1[:, 4:5], Alu.mult))
    vhop()
    op("vector", lambda e: e.tensor_tensor(sc1[:, 17:18], sc1[:, 16:17],
                                           sc1[:, 15:16], Alu.mult))
    vhop()
    op("vector", lambda e: e.tensor_scalar(sc1[:, 18:19], sc1[:, 17:18],
                                           1.0 / 32767.0, 1e-12,
                                           Alu.mult, Alu.max))
    vhop()
    op("vector", lambda e: e.reciprocal(sc1[:, 19:20], sc1[:, 18:19]))
    vhop()
    op("vector", lambda e: e.tensor_tensor(sc1[:, 20:21], sc1[:, 16:17],
                                           sc1[:, 19:20], Alu.mult),
       [("VE", 1)])
    vhop()
    bcast(sc1[:, 18:19], scB[:, 7:8])
    bcast(sc1[:, 20:21], scB[:, 8:9])
    scB_ve_b4 = S.now("VE")

    nc.sbuf_base = markP
    WdT = sb("WdT", [128, ICN * D], BF16)
    otl = [sb(f"otl{i}", [128, ICN * 256], F32) for i in range(2)]
    tol = [sb(f"tol{i}", [128, ICN * 128], F32) for i in range(2)]
    oql = [sb(f"oql{i}", [128, ICN * 128], BF16) for i in range(2)]
    x1f = [sb(f"x1f{i}", [128, D], F32) for i in range(4)]
    outst = [sb(f"outst{i}", [128, D], F32) for i in range(2)]

    qgate(mF3, ("sync", "scalar", "gpsimd"))
    wdT_s2 = wT_load("wdown", WdT, dq_gd)

    f4_se = [None, None]
    f4_ve = [None, None, None, None]
    oq_se_l = [None, None]
    mm_te = [None, None]
    out_st = [None, None]
    f4_lds = {}

    def f4_load(pp):
        buf = pp % 2
        if f4_se[buf] is not None:
            w8("sync", "SE", f4_se[buf])
        f4_lds["o", pp] = dma(
            "sync", otl[buf][:, :].rearrange("p (c t) -> p c t", c=ICN),
            sOT.rearrange("(c p) t -> p c t", p=128)
            [:, :, pp * 256:(pp + 1) * 256], "LD")
        for j in range(2):
            tt = 2 * pp + j
            if f4_ve[tt % 4] is not None:
                w8("sync", "VE", f4_ve[tt % 4])
            f4_lds["x", tt] = dma("sync", x1f[tt % 4][:, :],
                                  sX1[tt * 128:(tt + 1) * 128, :], "LD")

    def f4_scal(tt):
        pp, j = tt // 2, tt % 2
        buf = pp % 2
        ob = tt % 2
        w8("scalar", "LD", f4_lds["o", pp])
        if tt == 0:
            w8("scalar", "VE", scB_ve_b4)
        ot3 = (otl[buf][:, :].rearrange("p (c t) -> p c t", c=ICN)
               [:, :, j * 128:(j + 1) * 128])
        op("scalar", lambda e, b=buf, a=ot3, ob=ob: e.activation(
            tol[ob][:, :], a, Act.Identity, bias=0.0, scale=scB[:, 8:9]))
        if mm_te[ob] is not None:
            w8("scalar", "TE", mm_te[ob])
        op("scalar", lambda e, ob=ob: e.activation(
            oql[ob][:, :], tol[ob][:, :], Act.Identity, bias=0.0,
            scale=scB[:, 7:8]), [("SE", 1)])
        oq_se_l[ob] = S.now("SE")
        if j == 1:
            f4_se[buf] = S.now("SE")

    def f4_mm(tt):
        ob = tt % 2
        if tt == 0:
            w8("tensor", "S2", wdT_s2)
        w8("tensor", "SE", oq_se_l[ob])
        for h in range(2):
            bank, pbank = (("pA", pA) if h == 0 else ("pB", pB))
            bank_wait(bank)
            for icc in range(ICN):
                mm(pbank[:, 0:512],
                   oql[ob][:, icc * 128:(icc + 1) * 128],
                   WdT[:, icc * D + h * 512:icc * D + (h + 1) * 512],
                   icc == 0, icc == ICN - 1, inc=(icc == ICN - 1))
            w8("vector", "TE")
            if h == 0:
                w8("vector", "LD", f4_lds["x", tt])
                if out_st[ob] is not None:
                    w8("vector", "S2", out_st[ob])
            op("vector", lambda e, b=ob, x4=tt % 4, h=h, pb=pbank:
               e.tensor_tensor(
                   outst[b][:, h * 512:(h + 1) * 512], pb[:, 0:512],
                   x1f[x4][:, h * 512:(h + 1) * 512], Alu.add), [("VE", 1)])
            bank_done(bank, "VE")
        mm_te[ob] = S.now("TE")
        f4_ve[tt % 4] = S.now("VE")
        w8("scalar", "VE")
        out_st[ob] = dma("scalar", out_h[tt * 128:(tt + 1) * 128, :],
                         outst[ob][:, :], "S2")

    for t in range(NT128 + 2):
        if t % 2 == 0 and t // 2 < NT128 // 2:
            f4_load(t // 2)
        if 0 <= t - 1 < NT128:
            f4_scal(t - 1)
        if 0 <= t - 2 < NT128:
            f4_mm(t - 2)

    w8("sync", "S2")
    w8("gpsimd", "S2")

    with nc.Block() as block:
        def runner(name):
            def body(eng):
                for e, kind, payload in S.items:
                    if e != name:
                        continue
                    if kind == "wait":
                        semnm, val = payload
                        eng.wait_ge(sems[semnm], val)
                    else:
                        fn, incs = payload
                        inst = fn(eng)
                        for semnm, n in incs:
                            inst = inst.then_inc(sems[semnm], n)
            return body
        block.sync(runner("sync"))
        block.scalar(runner("scalar"))
        block.tensor(runner("tensor"))
        block.vector(runner("vector"))
        block.gpsimd(runner("gpsimd"))

    return nc


_NC_CACHE = {}


def kernel(L0_attn_x, L0_attn_Wq, L0_attn_bq, L0_attn_Wk, L0_attn_bk,
           L0_attn_Wv, L0_attn_bv, L0_attn_Wo, L0_attn_bo,
           L0_ffn_w_gate, L0_ffn_w_up, L0_ffn_w_down):
    if "nc" not in _NC_CACHE:
        _NC_CACHE["nc"] = build()
    nc = _NC_CACHE["nc"]
    f32 = np.float32

    def cvt(a):
        return np.ascontiguousarray(np.asarray(a, dtype=f32))

    X = cvt(L0_attn_x)
    base = {
        "wq": cvt(L0_attn_Wq), "wk": cvt(L0_attn_Wk), "wv": cvt(L0_attn_Wv),
        "wo": cvt(L0_attn_Wo), "wgate": cvt(L0_ffn_w_gate),
        "wup": cvt(L0_ffn_w_up), "wdown": cvt(L0_ffn_w_down),
        "bq": cvt(L0_attn_bq).reshape(1, D),
        "bk": cvt(L0_attn_bk).reshape(1, D),
        "bv": cvt(L0_attn_bv).reshape(1, D),
        "bo": cvt(L0_attn_bo).reshape(1, D),
    }
    in_maps = [dict(base, x=np.ascontiguousarray(X[c])) for c in range(8)]
    res = run_bass_kernel_spmd(nc, in_maps, core_ids=list(range(8)))
    out = np.stack([res.results[c]["out"] for c in range(8)], axis=0)
    return out.astype(np.float32)
